# version 25
# baseline (speedup 1.0000x reference)
import numpy as np
import ml_dtypes
import concourse.bass as bass
import concourse.mybir as mybir
from concourse.bass_utils import run_bass_kernel_spmd

F32 = mybir.dt.float32
BF16 = mybir.dt.bfloat16
I8 = mybir.dt.int8
AF = mybir.ActivationFunctionType
ALU = mybir.AluOpType
F32R = mybir.dt.float32r

T = 2048
D = 2048
NH = 16
DH = 128
IO = 8192
NJC = 16
QT = 512
NQT = T // QT
NIC = IO // 128
MSHIFT = -32.0
SCL = float(DH) ** -0.5
NEG = -1.0e30

DMA_SEMS = ("gq", "sq", "sxx", "wq", "wk", "wv", "wo0", "wo1", "xt0", "xt1",
            "gw0", "gw1", "uw0", "uw1", "dq0", "dq1", "dq2", "dq3", "out0", "out1")
ENGINES = ("tensor", "scalar", "vector", "gpsimd", "sync")


class Sched:
    def __init__(self, strict=False):
        self.ops = {e: [] for e in ENGINES}
        self.cnt = {}
        self.strict = strict

    def emit(self, eng, fn, deps=(), inc=None):
        deps2 = []
        for d in deps:
            if d is None:
                continue
            s, v = d
            if not self.strict and s == eng:
                continue
            if (s, eng) in (("gq", "gpsimd"), ("sq", "sync")):
                continue
            deps2.append((s, v))
        if self.strict and eng not in DMA_SEMS:
            if self.cnt.get(eng, 0) > 0:
                deps2.append((eng, self.cnt[eng]))
            if not inc:
                inc = True
        tok = None
        if inc:
            sem = eng if inc is True else inc
            a = 16 if sem in DMA_SEMS else 1
            self.cnt[sem] = self.cnt.get(sem, 0) + a
            tok = (sem, self.cnt[sem])
        self.ops[eng].append((deps2, fn, tok))
        return tok

    def replay(self, eng_name, eng, sems, wm):
        for deps, fn, tok in self.ops[eng_name]:
            coll = {}
            for s, v in deps:
                if v > coll.get(s, 0):
                    coll[s] = v
            for s, v in coll.items():
                if wm.get(s, 0) < v:
                    eng.wait_ge(sems[s], v)
                    wm[s] = v
            ins = fn(eng)
            if tok is not None:
                s, v = tok
                ins.then_inc(sems[s], 16 if s in DMA_SEMS else 1)


def build_nc(debug=False):
    import os
    strict = os.environ.get("BASS_SIM_FP64") == "1"
    nc = bass.Bass("TRN2", target_bir_lowering=False)

    xfm = nc.dram_tensor("xfm", [D, T], F32, kind="ExternalInput").ap()
    wqt = nc.dram_tensor("wqt", [D, D], F32, kind="ExternalInput").ap()
    wkt = nc.dram_tensor("wkt", [D, D], F32, kind="ExternalInput").ap()
    wvt = nc.dram_tensor("wvt", [D, D], F32, kind="ExternalInput").ap()
    wot = nc.dram_tensor("wot", [D, D], F32, kind="ExternalInput").ap()
    wg = nc.dram_tensor("wg", [D, IO], I8, kind="ExternalInput").ap()
    wu = nc.dram_tensor("wu", [D, IO], I8, kind="ExternalInput").ap()
    wdn = nc.dram_tensor("wdn", [IO, D], I8, kind="ExternalInput").ap()
    sgp_d = nc.dram_tensor("sgp", [128, 64], F32, kind="ExternalInput").ap()
    sup_d = nc.dram_tensor("sup", [128, 64], F32, kind="ExternalInput").ap()
    sdp_d = nc.dram_tensor("sdp", [128, 16], F32, kind="ExternalInput").ap()
    sdr_d = nc.dram_tensor("sdr", [1, D], F32, kind="ExternalInput").ap()
    out_d = nc.dram_tensor("out", [T, D], F32, kind="ExternalOutput").ap()
    wdnb = nc.dram_tensor("wdnb", [IO, D], BF16, kind="Internal").ap()

    th = 10000.0 ** (-2.0 * np.arange(DH // 2, dtype=np.float64) / DH)
    ang = np.arange(T, dtype=np.float64)[:, None] * th[None, :]
    cosv = np.cos(ang).astype(np.float32).T
    sinv = np.sin(ang).astype(np.float32).T
    cosF = np.empty((128, T), dtype=np.float32)
    sinF = np.empty((128, T), dtype=np.float32)
    cosF[0::2, :] = cosv
    cosF[1::2, :] = cosv
    sinF[0::2, :] = -sinv
    sinF[1::2, :] = sinv
    pswap = np.zeros((128, 128), dtype=np.float32)
    for i in range(64):
        pswap[2 * i + 1, 2 * i] = 1.0
        pswap[2 * i, 2 * i + 1] = 1.0
    tri = np.zeros((128, 128), dtype=np.float32)
    for p in range(128):
        tri[p, :p] = NEG
    ident = np.eye(128, dtype=np.float32)
    onescol = np.ones((128, 1), dtype=np.float32)
    onesrow = np.ones((1, 128), dtype=np.float32)
    cpool_v = np.array([MSHIFT, SCL, 1.0 / D, 1e-5, -0.5, -1.0, -1.0 / D, 0.0],
                       dtype=np.float32)[None, :].repeat(128, axis=0)

    cosF_d = nc.inline_tensor(np.ascontiguousarray(cosF), "cosFc").ap()
    sinF_d = nc.inline_tensor(np.ascontiguousarray(sinF), "sinFc").ap()
    pswap_d = nc.inline_tensor(pswap, "pswapc").ap()
    tri_d = nc.inline_tensor(tri, "tric").ap()
    ident_d = nc.inline_tensor(ident, "identc").ap()
    onescol_d = nc.inline_tensor(onescol, "onescolc").ap()
    onesrow_d = nc.inline_tensor(onesrow, "onesrowc").ap()
    cpool_d = nc.inline_tensor(np.ascontiguousarray(cpool_v), "cpoolc").ap()

    BASE = 16512
    R_A = BASE
    R_B = BASE + 65536
    R_W = BASE + 131072
    R_K = BASE + 196608
    at = nc.alloc_sbuf_tensor_at

    xbf = at("xbf", [128, NJC, T], BF16, offset=R_A)
    heads = at("heads", [128, NJC, T], BF16, offset=R_B)
    fuse = at("fuse", [128, NIC, QT], BF16, offset=R_B)

    cosf = at("cosf", [128, T], BF16, offset=R_W + 0)
    sinf = at("sinf", [128, T], BF16, offset=R_W + 4096)
    qraw = at("qraw", [128, T], BF16, offset=R_W + 8192)
    qr = at("qr", [128, 2, T], BF16, offset=R_W + 12288)
    kr = at("kr", [128, 2, T], BF16, offset=R_W + 20480)
    pt = at("pt", [128, 4, QT], BF16, offset=R_W + 28672)
    vtm = at("vtm", [128, 16, 256], BF16, offset=R_W + 32768)
    wqs = at("wqs", [128, NJC, 128], BF16, offset=R_W + 40960)
    wks = at("wks", [128, NJC, 128], BF16, offset=R_W + 45056)
    wvs = at("wvs", [128, NJC, 256], BF16, offset=R_W + 49152)
    rsc1 = at("rsc1", [128, 512], F32, offset=R_W + 57344)
    rsc2 = at("rsc2", [128, 512], F32, offset=R_W + 59392)
    sdrow = at("sdrow", [1, D], F32, offset=R_W + 57344)

    wo_init = at("wo_init", [128, 2, NJC, 128], BF16, offset=R_W + 0)
    ybf = at("ybf", [128, NJC, 512], BF16, offset=R_W + 0)
    ysq = at("ysq", [128, 512], BF16, offset=R_W + 16384)
    wos = at("wos", [128, 2, NJC, 128], BF16, offset=R_W + 17408)
    xtile = at("xtile", [128, 2, 512], F32, offset=R_W + 25600)
    m_sb = at("m_sb", [128, 512], BF16, offset=R_W + 29696)
    r_sb = at("r_sb", [128, 512], BF16, offset=R_W + 30720)

    dwn = at("dwn", [128, 4, 4, 512], BF16, offset=R_W + 0)
    y2t = at("y2t", [128, 2, D], BF16, offset=R_W + 16384)
    gws = at("gws", [128, 2, NJC, 256], BF16, offset=R_W + 24576)
    uws = at("uws", [128, 2, NJC, 256], BF16, offset=R_W + 40960)
    diag = at("diag", [128, 16, 128], BF16, offset=R_W + 57344)
    gsc = at("gsc", [128, 512], F32, offset=R_W + 61440)
    usc = at("usc", [128, 512], F32, offset=R_W + 63488)

    s_bc = at("s_bc", [128, D], F32, offset=R_K + 0)
    pswap_sb = at("pswap_sb", [128, 128], BF16, offset=R_K + 8192)
    tri_sb = at("tri_sb", [128, 128], BF16, offset=R_K + 8448)
    ident_sb = at("ident_sb", [128, 128], BF16, offset=R_K + 8704)
    onescol_sb = at("onescol_sb", [128, 1], BF16, offset=R_K + 8960)
    onesrow_sb = at("onesrow_sb", [1, 128], F32, offset=R_K + 8992)
    cpool_sb = at("cpool_sb", [128, 8], F32, offset=R_K + 9504)
    sg_sb = at("sg_sb", [128, 64], F32, offset=R_K + 9536)
    su_sb = at("su_sb", [128, 64], F32, offset=R_K + 9792)
    rowbuf = at("rowbuf", [1, QT], F32, offset=R_K + 10048)
    rowbuf2 = at("rowbuf2", [1, QT], F32, offset=R_K + 12096)
    recipbc = at("recipbc", [128, QT], F32, offset=R_K + 14144)
    bnsc = at("bnsc", [128, 24], F32, offset=R_K + 14144)
    bnmv = at("bnmv", [128, 4], F32, offset=R_K + 14240)
    small2 = at("small2", [128, 16], F32, offset=R_K + 14272)
    sdpp = at("sdpp", [128, 16], F32, offset=R_K + 14336)
    rsdp = at("rsdp", [128, 16], F32, offset=R_K + 14400)
    assert R_K + 16192 <= 229344

    PSD01 = nc.alloc_psum_tensor("psd01", [128, 1024], F32)
    PSD23 = nc.alloc_psum_tensor("psd23", [128, 1024], F32)
    PS = [PSD01[:, 0:512], PSD01[:, 512:1024],
          PSD23[:, 0:512], PSD23[:, 512:1024]] + [
        nc.alloc_psum_tensor("psb%d" % i, [128, 512], F32)[...] for i in range(4, 8)]
    PSDP = [PSD01, PSD23]

    if debug:
        dbg_x1 = nc.dram_tensor("dbg_x1", [128, NJC, T], F32, kind="ExternalOutput").ap()
        dbg_heads = nc.dram_tensor("dbg_heads", [128, NJC, T], F32, kind="ExternalOutput").ap()
        dbg_fuse = nc.dram_tensor("dbg_fuse", [128, NIC, QT], F32, kind="ExternalOutput").ap()
        dbg_y2 = nc.dram_tensor("dbg_y2", [16, 128, D], F32, kind="ExternalOutput").ap()

    S = Sched(strict=strict)

    def cp(idx, p=128):
        return cpool_sb[0:p, idx:idx + 1]

    xfm_r = xfm.rearrange("(jc p) t -> p jc t", p=128)
    wqt_r = wqt.rearrange("(jc p) i -> p jc i", p=128)
    wkt_r = wkt.rearrange("(jc p) i -> p jc i", p=128)
    wvt_r = wvt.rearrange("(jc p) i -> p jc i", p=128)
    wot_r = wot.rearrange("(jc p) i -> p jc i", p=128)
    wg_r = wg.rearrange("(hc p) io -> p hc io", p=128)
    wu_r = wu.rearrange("(hc p) io -> p hc io", p=128)
    wdn_r = wdn.rearrange("(icb ic p) ho -> p icb ic ho", p=128, ic=4)
    wdnb_r = wdnb.rearrange("(icb ic p) ho -> p icb ic ho", p=128, ic=4)

    t_cn = {}
    for nm, dst, src in (("cos", cosf, cosF_d), ("sin", sinf, sinF_d),
                         ("psw", pswap_sb, pswap_d), ("tri", tri_sb, tri_d),
                         ("idn", ident_sb, ident_d), ("onc", onescol_sb, onescol_d)):
        t_cn[nm] = S.emit("gpsimd", lambda g, a=dst, b=src: g.dma_start(a[...], b), inc="gq")
    for nm in ("cos", "sin", "psw", "tri", "idn", "onc"):
        t_cn[nm] = t_cn["onc"]
    for nm, dst, src in (("onr", onesrow_sb, onesrow_d), ("cpl", cpool_sb, cpool_d),
                         ("sg", sg_sb, sgp_d), ("su", su_sb, sup_d), ("sdr", sdrow, sdr_d)):
        t_cn[nm] = S.emit("sync", lambda s, a=dst, b=src: s.dma_start(a[...], b), inc="sq")
    for nm in ("onr", "cpl", "sg", "su", "sdr"):
        t_cn[nm] = t_cn["sdr"]

    t_x = []
    for j4 in range(4):
        t_x.append(S.emit("gpsimd",
                          lambda g, j4=j4: g.dma_start(xbf[:, j4 * 4:(j4 + 1) * 4, :],
                                                       xfm_r[:, j4 * 4:(j4 + 1) * 4, :]),
                          inc="sxx"))
    t_xc = t_x[-1]

    t_sbc = None
    for n4 in range(4):
        tm = S.emit("tensor",
                    lambda t, n4=n4: t.matmul(PS[6], onesrow_sb[:, :],
                                              sdrow[0:1, n4 * 512:(n4 + 1) * 512],
                                              start=True, stop=True, skip_group_check=True),
                    deps=[t_cn["onr"], t_cn["sdr"], t_sbc], inc=True)
        t_sbc = S.emit("scalar",
                       lambda s, n4=n4: s.copy(s_bc[:, n4 * 512:(n4 + 1) * 512], PS[6][:, :]),
                       deps=[tm], inc=True)

    def load_wslice(dst_ap, src_r, c0, c1, deps, sem="gq"):
        tk = None
        for h2 in range(2):
            tk = S.emit("gpsimd",
                        lambda g, h2=h2, dst_ap=dst_ap, src_r=src_r, c0=c0, c1=c1:
                        g.dma_start(dst_ap[:, h2 * 8:(h2 + 1) * 8, :],
                                    src_r[:, h2 * 8:(h2 + 1) * 8, c0:c1]),
                        deps=deps if h2 == 0 else [], inc=sem)
        return tk

    st = {
        "projbank": None,
        "ps5_rd": None,
        "rope": {},
        "lastS": {},
        "wq": {}, "wk": {}, "wv": {},
        "qk_mm_done": {},
        "v_mm_done": {},
        "vtm_ev": {},
        "exp_bank": {},
        "pt_rd": {},
        "sums_free": None,
        "rb2_rd": None,
        "ps6_rd": None,
        "recip_rd": None,
        "pv_free": None,
        "qraw_rd": None,
        "hevac": {},
        "rsc_rd": None,
    }

    st["wq"][0] = load_wslice(wqs, wqt_r, 0, 128, [t_cn["sdr"]], sem="wq")
    st["wk"][0] = load_wslice(wks, wkt_r, 0, 128, [], sem="wk")
    st["wv"][0] = load_wslice(wvs, wvt_r, 0, 256, [t_sbc], sem="wv")

    def emit_qk_proj(h, which):
        par = h % 2
        dst = qr if which == "q" else kr
        w_sb = wqs if which == "q" else wks
        t_w = st["wq" if which == "q" else "wk"][h]
        last_mm = None
        for tt in range(4):
            sl = slice(tt * 512, (tt + 1) * 512)
            tmm = None
            for jc in range(NJC):
                d = []
                if jc == 0:
                    d = [t_w, st["projbank"], t_xc]
                tmm = S.emit("tensor",
                             lambda t, jc=jc, sl=sl, w_sb=w_sb:
                             t.matmul(PS[7][:, :], w_sb[:, jc, :], xbf[:, jc, sl],
                                      start=(jc == 0), stop=(jc == NJC - 1),
                                      skip_group_check=True),
                             deps=d, inc=(jc == NJC - 1))
            last_mm = tmm
            tev = S.emit("vector",
                         lambda v, sl=sl: v.tensor_copy(qraw[:, sl], PS[7]),
                         deps=[tmm, st["qraw_rd"]], inc=True)
            st["projbank"] = tev
            tqs = S.emit("tensor",
                         lambda t, sl=sl: t.matmul(PS[3][:, :], pswap_sb[:, :], qraw[:, sl],
                                                   start=True, stop=True,
                                                   skip_group_check=True),
                         deps=[tev, st["ps5_rd"], st["exp_bank"].get(1),
                               t_cn["psw"]], inc=True)
            ta = S.emit("vector",
                        lambda v, sl=sl: v.tensor_tensor(rsc1[:, :], qraw[:, sl], cosf[:, sl],
                                                         ALU.mult),
                        deps=[tev, t_cn["cos"], st["rsc_rd"], t_sbc], inc=True)
            tb = S.emit("vector",
                        lambda v, sl=sl: v.tensor_tensor(rsc2[:, :], PS[3][:, :], sinf[:, sl],
                                                         ALU.mult),
                        deps=[tqs, t_cn["sin"]], inc=True)
            st["ps5_rd"] = tb
            tc_ = S.emit("vector",
                         lambda v, sl=sl, dst=dst, par=par:
                         v.tensor_tensor(dst[:, par, sl], rsc1[:, :], rsc2[:, :], ALU.add),
                         deps=[st["lastS"].get(h - 2)], inc=True)
            st["rsc_rd"] = tc_
            st["qraw_rd"] = tc_
            st["rope"][(h, which, tt)] = tc_
        st["qk_mm_done"][h] = ("tensor", S.cnt["tensor"])
        return last_mm

    def emit_prep(h):
        emit_qk_proj(h, "q")
        emit_qk_proj(h, "k")
        if h + 1 < NH:
            st["wq"][h + 1] = load_wslice(wqs, wqt_r, (h + 1) * 128, (h + 2) * 128,
                                          [st["qk_mm_done"][h]], sem="wq")
            st["wk"][h + 1] = load_wslice(wks, wkt_r, (h + 1) * 128, (h + 2) * 128,
                                          [st["qk_mm_done"][h]], sem="wk")


    def emit_vproj(g):
        for tcc in range(16):
            tmm = None
            for jc in range(NJC):
                d = []
                if jc == 0:
                    d = [st["wv"][g], t_xc, st["projbank"]]
                tmm = S.emit("tensor",
                             lambda t, jc=jc, tcc=tcc:
                             t.matmul(PS[7][:, 0:256], xbf[:, jc, tcc * 128:(tcc + 1) * 128],
                                      wvs[:, jc, :],
                                      start=(jc == 0), stop=(jc == NJC - 1),
                                      skip_group_check=True),
                             deps=d, inc=(jc == NJC - 1))
            tev = S.emit("vector",
                         lambda v, tcc=tcc: v.tensor_copy(vtm[:, tcc, :], PS[7][:, 0:256]),
                         deps=[tmm], inc=True)
            st["projbank"] = tev
        st["v_mm_done"][g] = ("tensor", S.cnt["tensor"])
        st["vtm_ev"][g] = ("vector", S.cnt["vector"])
        if g + 1 < 8:
            st["wv"][g + 1] = load_wslice(wvs, wvt_r, (g + 1) * 256, (g + 2) * 256,
                                          [st["v_mm_done"][g]], sem="wv")

    def emit_attn(h):
        par = h % 2
        g = h // 2
        off = (h % 2) * 128
        for qt in range(NQT):
            nchunks = 4 * qt + 4
            qsl = slice(qt * 512, (qt + 1) * 512)
            sums_ps = PS[5]
            t_S = {}
            t_tri = {}
            t_sum_last = None
            t_pv_last = None
            pending = {}

            def emit_S_pair(cp_i):
                cc = [2 * cp_i, 2 * cp_i + 1]
                slots = [c % 4 for c in cc]
                pp = cp_i % 2
                for c in cc:
                    bank = 2 * pp + (c % 2)
                    d = [st["rope"][(h, "k", c // 4)], st["exp_bank"].get(pp)]
                    if pp == 1 and c % 2 == 0:
                        d.append(st["ps5_rd"])
                    if c == 0:
                        d += [st["rope"][(h, "q", qt)], st["vtm_ev"][g]]
                    t_S[c] = S.emit("tensor",
                                    lambda t, c=c, bank=bank, par=par, qsl=qsl:
                                    t.matmul(PS[bank],
                                             kr[:, par, c * 128:(c + 1) * 128],
                                             qr[:, par, qsl],
                                             start=True, stop=True, skip_group_check=True),
                                    deps=d, inc=(c % 2 == 1))
                for c in cc:
                    if c >= 4 * qt:
                        bank = 2 * pp + (c % 2)
                        c0 = (c - 4 * qt) * 128
                        t_tri[c] = S.emit("vector",
                                          lambda v, c=c, bank=bank, c0=c0:
                                          v.tensor_tensor(PS[bank][:, c0:c0 + 128],
                                                          PS[bank][:, c0:c0 + 128],
                                                          tri_sb[:, :], ALU.add),
                                          deps=[t_S[cc[1]], t_cn["tri"]], inc=True)
                d = [t_S[cc[1]], t_tri.get(cc[0]), t_tri.get(cc[1]),
                     st["pt_rd"].get(slots[0]), st["pt_rd"].get(slots[1]), t_cn["cpl"]]
                t_exp = S.emit("scalar",
                               lambda s, s0=slots[0], pp=pp:
                               s.activation(pt[:, s0:s0 + 2, :], PSDP[pp][:, 0:1024],
                                            AF.Exp, bias=cp(0), scale=cp(1)),
                               deps=d, inc=True)
                st["exp_bank"][pp] = t_exp
                t_ms = {}
                for c in cc:
                    c0 = (c - 4 * qt) * 128
                    if c >= 4 * qt and c0 > 0:
                        t_ms[c] = S.emit("vector",
                                         lambda v, slot=c % 4, c0=c0:
                                         v.memset(pt[:, slot, 0:c0], 0.0),
                                         deps=[t_exp], inc=True)
                pending[cp_i] = (cc, t_exp, t_ms)

            def emit_consume_pair(cp_i):
                nonlocal t_sum_last, t_pv_last
                cc, t_exp, t_ms = pending.pop(cp_i)
                for c in cc:
                    slot = c % 4
                    d = [t_exp, t_ms.get(c)]
                    if c == 0:
                        d.append(st["sums_free"])
                    t_sum_last = S.emit("tensor",
                                        lambda t, c=c, slot=slot, sums_ps=sums_ps,
                                        nchunks=nchunks:
                                        t.matmul(sums_ps[0:1, :], onescol_sb[:, :],
                                                 pt[:, slot, :],
                                                 start=(c == 0), stop=(c == nchunks - 1),
                                                 skip_group_check=True),
                                        deps=d + [t_cn["onc"]], inc=(c == nchunks - 1))
                    d = []
                    if c == 0:
                        d = [st["pv_free"]]
                    t_pv_last = S.emit("tensor",
                                       lambda t, c=c, slot=slot, off=off, nchunks=nchunks:
                                       t.matmul(PS[4], vtm[:, c, off:off + 128],
                                                pt[:, slot, :],
                                                start=(c == 0), stop=(c == nchunks - 1),
                                                skip_group_check=True),
                                       deps=d, inc=True)
                    st["pt_rd"][slot] = t_pv_last

            for cp_i in range(nchunks // 2):
                emit_S_pair(cp_i)
                if cp_i == 0 and st.get("pend_chain"):
                    st.pop("pend_chain")()
                if cp_i > 0:
                    emit_consume_pair(cp_i - 1)
            emit_consume_pair(nchunks // 2 - 1)
            st["lastS"][h] = ("tensor", S.cnt["tensor"])
            t_ln = S.emit("scalar",
                          lambda s, sums_ps=sums_ps: s.activation(rowbuf[0:1, :],
                                                                  sums_ps[0:1, :], AF.Ln),
                          deps=[t_sum_last], inc=True)
            st["sums_free"] = t_ln
            t_rc = S.emit("scalar",
                          lambda s: s.activation(rowbuf2[0:1, :], rowbuf[0:1, :],
                                                 AF.Exp, scale=cp(5, 1)),
                          deps=[st["rb2_rd"]], inc=True)

            def mk_chain(h, qt, qsl, t_rc, t_pv_last):
                def emit_chain():
                    t_bc = S.emit("tensor",
                                  lambda t: t.matmul(PS[6], onesrow_sb[:, :],
                                                     rowbuf2[0:1, :],
                                                     start=True, stop=True,
                                                     skip_group_check=True),
                                  deps=[t_rc, st["ps6_rd"]], inc=True)
                    st["rb2_rd"] = t_bc
                    t_cpy = S.emit("vector",
                                   lambda v: v.tensor_copy(recipbc[:, :], PS[6]),
                                   deps=[t_bc, st["recip_rd"]], inc=True)
                    st["ps6_rd"] = t_cpy
                    t_hv = S.emit("vector",
                                  lambda v, h=h, qsl=qsl:
                                  v.tensor_tensor(heads[:, h, qsl], PS[4], recipbc[:, :],
                                                  ALU.mult),
                                  deps=[t_pv_last, t_cpy], inc=True)
                    st["recip_rd"] = t_hv
                    st["pv_free"] = t_hv
                    st["hevac"][(h, qt)] = t_hv
                return emit_chain
            st["pend_chain"] = mk_chain(h, qt, qsl, t_rc, t_pv_last)

    st["ps6_rd"] = t_sbc
    emit_prep(0)
    emit_vproj(0)
    for h in range(NH):
        if h + 1 < NH:
            emit_prep(h + 1)
        emit_attn(h)
        if h % 2 == 1 and h // 2 + 1 < 8:
            emit_vproj(h // 2 + 1)

    if st.get("pend_chain"):
        st.pop("pend_chain")()
    t_attn = st["hevac"][(NH - 1, NQT - 1)]

    t_wo_ld = {}
    t_wo_ld[0] = load_wslice(wo_init[:, 0], wot_r, 0, 128,
                             [st["rope"][(NH - 1, "k", 3)]], sem="wo0")
    t_wo_ld[1] = load_wslice(wo_init[:, 1], wot_r, 128, 256, [], sem="wo1")
    t_xt = {}
    t_womm = {}
    t_yadd = {}
    t_stats = {}
    ln1 = {"ps6_rd": st["ps6_rd"], "ps7_rd": st["projbank"], "x1": {},
           "m_rd": None, "r_rd": st["rb2_rd"], "mcp": None, "rcp": None,
           "pending": None}

    for tt in range(4):
        tsl = slice(tt * 512, (tt + 1) * 512)
        t_cs = None
        t_sq = None
        for ic in range(NJC):
            n = tt * 16 + ic
            par = ic % 2
            bank = 4 + par
            if ln1["pending"] is not None:
                ln1["pending"](ic)
            t_xt[(tt, ic)] = S.emit("sync",
                                    lambda s, ic=ic, par=par, tsl=tsl:
                                    s.dma_start(xtile[:, par, :], xfm_r[:, ic, tsl]),
                                    deps=[t_attn if n < 2 else None,
                                          t_yadd.get((tt, ic - 2)) or
                                          (t_yadd.get((tt - 1, 14 + ic)) if tt > 0 else None)],
                                    inc="xt%d" % par)
            tmm = None
            for jc in range(NJC):
                d = []
                if jc == 0:
                    d = [t_wo_ld[n], t_attn, st["sums_free"],
                         t_yadd.get((tt, ic - 2)) or
                         (t_yadd.get((tt - 1, 14 + ic)) if tt > 0 else None)]
                wsrc = wo_init if n < 2 else wos
                tmm = S.emit("tensor",
                             lambda t, jc=jc, ic=ic, bank=bank, tsl=tsl, wsrc=wsrc:
                             t.matmul(PS[bank][:, :], wsrc[:, ic % 2, jc, :],
                                      heads[:, jc, tsl],
                                      start=(jc == 0), stop=(jc == NJC - 1),
                                      skip_group_check=True),
                             deps=d, inc=(jc == NJC - 1))
            t_womm[(tt, ic)] = tmm
            if n + 2 < 64:
                nic = (n + 2) % 16
                t_wo_ld[n + 2] = load_wslice(wos[:, (n + 2) % 2], wot_r,
                                             nic * 128, (nic + 1) * 128, [tmm],
                                             sem="wo%d" % ((n + 2) % 2))
            t_yadd[(tt, ic)] = S.emit("vector",
                                      lambda v, ic=ic, par=par, bank=bank:
                                      v.tensor_tensor(ybf[:, ic, :], PS[bank][:, :],
                                                      xtile[:, par, :], ALU.add),
                                      deps=[tmm, t_xt[(tt, ic)]],
                                      inc=True)
            t_ysq = S.emit("vector",
                           lambda v, ic=ic: v.tensor_tensor(ysq[:, :], ybf[:, ic, :],
                                                            ybf[:, ic, :], ALU.mult),
                           deps=[t_stats.get("sq")], inc=True)
            t_cs = S.emit("tensor",
                          lambda t, ic=ic: t.matmul(PS[6][0:1, :], onescol_sb[:, :],
                                                    ybf[:, ic, :],
                                                    start=(ic == 0), stop=(ic == NJC - 1),
                                                    skip_group_check=True),
                          deps=[t_yadd[(tt, ic)]] + ([ln1["ps6_rd"]] if ic == 0 else []),
                          inc=(ic == NJC - 1))
            t_sq = S.emit("tensor",
                          lambda t, ic=ic: t.matmul(PS[7][0:1, :], onescol_sb[:, :],
                                                    ysq[:, :],
                                                    start=(ic == 0), stop=(ic == NJC - 1),
                                                    skip_group_check=True),
                          deps=[t_ysq] + ([ln1["ps7_rd"]] if ic == 0 else []),
                          inc=(ic == NJC - 1))
            t_stats["sq"] = t_sq
        t_m = S.emit("scalar",
                     lambda s: s.activation(rowbuf[0:1, :], PS[6][0:1, :], AF.Copy,
                                            scale=cp(2, 1)),
                     deps=[t_cs, ln1["m_rd"]], inc=True)
        t_msq = S.emit("scalar",
                       lambda s: s.activation(rowbuf2[0:1, :], PS[7][0:1, :], AF.Copy,
                                              scale=cp(2, 1)),
                       deps=[t_sq, ln1["r_rd"]], inc=True)
        t_m2 = S.emit("vector",
                      lambda v: v.tensor_tensor(recipbc[0:1, :], rowbuf[0:1, :],
                                                rowbuf[0:1, :], ALU.mult),
                      deps=[t_m, st["recip_rd"]], inc=True)
        t_var = S.emit("vector",
                       lambda v: v.tensor_tensor(rowbuf2[0:1, :], rowbuf2[0:1, :],
                                                 recipbc[0:1, :], ALU.subtract),
                       deps=[t_msq], inc=True)
        t_lnv = S.emit("scalar",
                       lambda s: s.activation(recipbc[0:1, :], rowbuf2[0:1, :], AF.Ln,
                                              bias=cp(3, 1)),
                       deps=[t_var, t_m2], inc=True)
        t_rstd = S.emit("scalar",
                        lambda s: s.activation(rowbuf2[0:1, :], recipbc[0:1, :], AF.Exp,
                                               scale=cp(4, 1)),
                        deps=[t_lnv], inc=True)
        t_mbc = S.emit("tensor",
                       lambda t: t.matmul(PS[2], onesrow_sb[:, :], rowbuf[0:1, :],
                                          start=True, stop=True, skip_group_check=True),
                       deps=[t_m, t_m2,
                             (t_attn if tt == 0 else ln1["mcp"])],
                       inc=True)
        ln1["m_rd"] = t_mbc
        t_rbc = S.emit("tensor",
                       lambda t: t.matmul(PS[3], onesrow_sb[:, :], rowbuf2[0:1, :],
                                          start=True, stop=True, skip_group_check=True),
                       deps=[t_rstd,
                             (t_attn if tt == 0 else ln1["rcp"])],
                       inc=True)
        ln1["r_rd"] = t_rbc
        st["recip_rd"] = t_lnv
        t_mcp = S.emit("vector", lambda v: v.tensor_copy(m_sb[:, :], PS[2]),
                       deps=[t_mbc], inc=True)
        ln1["mcp"] = t_mcp
        t_rcp = S.emit("vector", lambda v: v.tensor_copy(r_sb[:, :], PS[3]),
                       deps=[t_rbc], inc=True)
        ln1["rcp"] = t_rcp

        def mk_x1(tt, tsl):
            def emit_x1(ic):
                S.emit("vector",
                       lambda v, ic=ic: v.tensor_tensor(xbf[:, ic, tsl], ybf[:, ic, :],
                                                        m_sb[:, :], ALU.subtract),
                       deps=[], inc=True)
                ln1["x1"][(tt, ic)] = S.emit(
                    "vector",
                    lambda v, ic=ic: v.tensor_tensor(xbf[:, ic, tsl], xbf[:, ic, tsl],
                                                     r_sb[:, :], ALU.mult),
                    deps=[], inc=True)
            return emit_x1
        ln1["pending"] = mk_x1(tt, tsl)
        if tt == 3:
            for ic in range(NJC):
                ln1["pending"](ic)
        ln1["ps6_rd"] = t_m
        ln1["ps7_rd"] = t_msq
    t_ln1_done = ln1["x1"][(3, 15)]
    t_wo_mm_all = ("tensor", S.cnt["tensor"])

    if debug:
        S.emit("gpsimd", lambda g: g.dma_start(dbg_heads[...], heads[...]),
               deps=[t_attn], inc="gq")
        S.emit("gpsimd", lambda g: g.dma_start(dbg_x1[...], xbf[...]),
               deps=[t_ln1_done], inc="gq")

    t_sdp = S.emit("sync", lambda s: s.dma_start(sdpp[...], sdp_d),
                   deps=[t_ln1_done, st["recip_rd"]], inc="sq")
    t_lnr = S.emit("scalar", lambda s: s.activation(rsdp[:, :], sdpp[:, :], AF.Ln),
                   deps=[t_sdp, t_ln1_done], inc=True)
    t_rsd = S.emit("scalar", lambda s: s.activation(sdpp[:, :], rsdp[:, :], AF.Exp,
                                                    scale=cp(5)),
                   deps=[t_lnr], inc=True)
    t_diag = None
    for dc in range(16):
        t_diag = S.emit("vector",
                        lambda v, dc=dc: v.tensor_scalar_mul(diag[:, dc, :], ident_sb[:, :],
                                                             sdpp[:, dc:dc + 1]),
                        deps=[t_rsd, t_cn["idn"], t_womm[(3, 15)], t_ln1_done], inc=True)

    ph2 = {"gw": {}, "uw": {},
           "silu": {}, "usc": {}, "fuse": {}, "dw_rd": {},
           "y2e": {}, "outt": {},
           "bank_rd": {0: t_ln1_done, 1: t_ln1_done}}

    def load_gu(which, s_idx, deps):
        src = wg_r if which == "g" else wu_r
        dst = gws if which == "g" else uws
        par = s_idx % 2
        sem = ("gw%d" if which == "g" else "uw%d") % par
        tk = None
        for h2 in range(2):
            tk = S.emit("gpsimd",
                        lambda g, h2=h2, dst=dst, src=src, s_idx=s_idx, par=par:
                        g.dma_start(dst[:, par, h2 * 8:(h2 + 1) * 8, :],
                                    src[:, h2 * 8:(h2 + 1) * 8,
                                        s_idx * 256:(s_idx + 1) * 256]),
                        deps=deps if h2 == 0 else [], inc=sem)
        return tk

    ph2["gw"][0] = load_gu("g", 0, [t_ln1_done])
    ph2["uw"][0] = load_gu("u", 0, [t_ln1_done])
    ph2["gw"][1] = load_gu("g", 1, [t_ln1_done])
    ph2["uw"][1] = load_gu("u", 1, [t_ln1_done])

    t_out_toks = []
    for tt in range(4):
        tsl = slice(tt * 512, (tt + 1) * 512)
        if tt > 0:
            for nx2, (tg_, tu_) in ph2.pop("nextgw", {}).items():
                ph2["gw"][nx2] = tg_
                ph2["uw"][nx2] = tu_
        for s_idx in range(32):
            for icw in range(2):
                ic = s_idx * 2 + icw
                tg = None
                for hc in range(NJC):
                    d = []
                    if hc == 0:
                        d = [ph2["gw"].get(s_idx) if icw == 0 else None, t_ln1_done,
                             ph2["silu"].get(ic - 1, ph2.get("last_silu"))]
                    tg = S.emit("tensor",
                                lambda t, hc=hc, ic=ic, s_idx=s_idx, icw=icw, tsl=tsl:
                                t.matmul(PS[0][:, :], gws[:, s_idx % 2, hc, icw * 128:(icw + 1) * 128],
                                         xbf[:, hc, tsl],
                                         start=(hc == 0), stop=(hc == NJC - 1),
                                         skip_group_check=True),
                                deps=d, inc=(hc == NJC - 1))
                if icw == 1:
                    ph2["g_mm_last"] = tg
                ph2["silu"][ic] = S.emit("scalar",
                                         lambda s, ic=ic:
                                         s.activation(gsc[:, :], PS[0][:, :],
                                                      AF.Silu,
                                                      scale=sg_sb[:, ic:ic + 1]),
                                         deps=[tg, ph2["fuse"].get(ic - 1), t_cn["sg"],
                                               (t_wo_mm_all if ic == 0 and tt == 0 else None)],
                                         inc=True)
                tu = None
                for hc in range(NJC):
                    d = []
                    if hc == 0:
                        d = [ph2["uw"].get(s_idx) if icw == 0 else None,
                             ph2["usc"].get(ic - 1, ph2.get("last_usc"))]
                    tu = S.emit("tensor",
                                lambda t, hc=hc, icw=icw, tsl=tsl, s_idx=s_idx:
                                t.matmul(PS[1][:, :], uws[:, s_idx % 2, hc, icw * 128:(icw + 1) * 128],
                                         xbf[:, hc, tsl],
                                         start=(hc == 0), stop=(hc == NJC - 1),
                                         skip_group_check=True),
                                deps=d, inc=(hc == NJC - 1))
                if icw == 1:
                    ph2["u_mm_last"] = tu
                ph2["usc"][ic] = S.emit("vector",
                                        lambda v, ic=ic:
                                        v.tensor_scalar_mul(usc[:, :], PS[1][:, :],
                                                            su_sb[:, ic:ic + 1]),
                                        deps=[tu, ph2["fuse"].get(ic - 1), t_cn["su"],
                                              (t_diag if ic == 0 and tt == 0 else None)],
                                        inc=True)
                ph2["fuse"][ic] = S.emit("vector",
                                         lambda v, ic=ic:
                                         v.tensor_tensor(fuse[:, ic, :], gsc[:, :],
                                                         usc[:, :], ALU.mult),
                                         deps=[ph2["silu"][ic],
                                               (t_wo_mm_all if (ic == 0 and tt == 0)
                                                else None),
                                               (ph2.get("dn_done_prev") if tt > 0
                                                else None)],
                                         inc=True)
            nxt = s_idx + 2
            if nxt < 32:
                ph2["gw"][nxt] = load_gu("g", nxt, [ph2["g_mm_last"]])
                ph2["uw"][nxt] = load_gu("u", nxt, [ph2["u_mm_last"]])
            else:
                nx2 = nxt - 32
                if tt < 3:
                    ph2["nextgw"] = ph2.get("nextgw", {})
                    ph2["nextgw"][nx2] = (load_gu("g", nx2, [ph2["g_mm_last"]]),
                                          load_gu("u", nx2, [ph2["u_mm_last"]]))
        ph2["last_silu"] = ph2["silu"][63]
        ph2["last_usc"] = ph2["usc"][63]
        t_fuse_all = ph2["fuse"][63]
        for tcb in range(2):
            for hos in range(4):
                t_dn_stop = {}
                for icb in range(16):
                    ph2["dwctr"] = ph2.get("dwctr", 0) + 1
                    par = ph2["dwctr"] % 4
                    t_dw = S.emit("gpsimd",
                                  lambda g, par=par, icb=icb, hos=hos:
                                  g.dma_start(dwn[:, par, :, :],
                                              wdn_r[:, icb, :, hos * 512:(hos + 1) * 512]),
                                  deps=[ph2["dw_rd"].get(par, t_ln1_done)],
                                  inc="dq%d" % par)
                    if icb == 12 and hos == 0 and ph2.get("pend_out"):
                        for f in ph2.pop("pend_out"):
                            f()
                    for icw in range(4):
                        ic = icb * 4 + icw
                        for tc2 in range(2):
                            tc = tcb * 2 + tc2
                            d = []
                            if icb == 0 and icw == 0:
                                d = [t_fuse_all, t_dw,
                                     ph2["bank_rd"].get(tc2)]
                            elif icw == 0:
                                d = [t_dw]
                            t_mm = S.emit("tensor",
                                          lambda t, ic=ic, icw=icw, tc=tc, tc2=tc2, par=par,
                                          icb=icb:
                                          t.matmul(PS[2 + tc2][:, :],
                                                   fuse[:, ic, tc * 128:(tc + 1) * 128],
                                                   dwn[:, par, icw, :],
                                                   start=(icb == 0 and icw == 0), stop=False,
                                                   skip_group_check=True),
                                          deps=d, inc=(icw == 3 and tc2 == 1))
                    ph2["dw_rd"][par] = t_mm
                for tc2 in range(2):
                    tc = tcb * 2 + tc2
                    for i_dc in range(4):
                        dc = hos * 4 + i_dc
                        t_dn_stop[tc2] = S.emit(
                            "tensor",
                            lambda t, tc2=tc2, tc=tc, dc=dc, i_dc=i_dc, tt=tt:
                            t.matmul(PS[2 + tc2][:, i_dc * 128:(i_dc + 1) * 128],
                                     xbf[:, dc, tt * 512 + tc * 128: tt * 512 + (tc + 1) * 128],
                                     diag[:, dc, :],
                                     start=False, stop=(i_dc == 3),
                                     skip_group_check=True),
                            deps=([t_diag, t_ln1_done] if i_dc == 0 else []),
                            inc=(i_dc == 3))
                for tc2 in range(2):
                    prev_pair = (tcb - 1, tc2) if tcb > 0 else (1, tc2)
                    ph2["y2e"][(tc2, hos)] = S.emit(
                        "vector",
                        lambda v, tc2=tc2, hos=hos:
                        v.tensor_tensor(y2t[:, tc2, hos * 512:(hos + 1) * 512],
                                        PS[2 + tc2][:, :], s_bc[:, hos * 512:(hos + 1) * 512],
                                        ALU.mult),
                        deps=[t_dn_stop[tc2],
                              (ph2["outt"].get(prev_pair) if hos == 0 else None)],
                        inc=True)
                    ph2["bank_rd"][tc2] = ph2["y2e"][(tc2, hos)]
            for tc2 in range(2):
                tc = tcb * 2 + tc2
                mv = tc2 * 2
                sbase = tc2 * 8
                t_sq4 = None
                for seg in range(4):
                    t_sq4 = S.emit("scalar",
                                   lambda s, tc2=tc2, seg=seg, sbase=sbase:
                                   s.activation(PS[4][:, :],
                                                y2t[:, tc2, seg * 512:(seg + 1) * 512],
                                                AF.Square,
                                                accum_out=bnsc[:, sbase + seg:sbase + seg + 1]),
                                   deps=([ph2["y2e"][(tc2, s_)] for s_ in range(4)]
                                         + [ph2.get(("sq_rd", tc2))]) if seg == 0 else [],
                                   inc=(seg == 3))
                t_rsum = S.emit("vector",
                                lambda v, tc2=tc2, mv=mv:
                                v.reduce_sum(bnmv[:, mv:mv + 1], y2t[:, tc2, :],
                                             axis=mybir.AxisListType.X),
                                deps=[ph2.get(("lv", tc2))], inc=True)
                t_ng = S.emit("vector",
                              lambda v, sbase=sbase, mv=mv:
                              v.tensor_scalar(small2[:, sbase:sbase + 1],
                                              bnmv[:, mv:mv + 1], cp(6), None, ALU.mult),
                              deps=[], inc=True)
                t_ssq = S.emit("vector",
                               lambda v, sbase=sbase, mv=mv:
                               v.reduce_sum(bnmv[:, mv + 1:mv + 2],
                                            bnsc[:, sbase:sbase + 4],
                                            axis=mybir.AxisListType.X),
                               deps=[t_sq4], inc=True)
                ph2[("sq_rd", tc2)] = t_ssq
                t_m2 = S.emit("vector",
                              lambda v, sbase=sbase:
                              v.tensor_tensor(small2[:, sbase + 3:sbase + 4],
                                              small2[:, sbase:sbase + 1],
                                              small2[:, sbase:sbase + 1], ALU.mult),
                              deps=[], inc=True)
                t_var = S.emit("vector",
                               lambda v, sbase=sbase, mv=mv:
                               v.scalar_tensor_tensor(small2[:, sbase + 2:sbase + 3],
                                                      bnmv[:, mv + 1:mv + 2], cp(2),
                                                      small2[:, sbase + 3:sbase + 4],
                                                      ALU.mult, ALU.subtract),
                               deps=[], inc=True)
                t_lv = S.emit("scalar",
                              lambda s, sbase=sbase:
                              s.activation(small2[:, sbase + 4:sbase + 5],
                                           small2[:, sbase + 2:sbase + 3],
                                           AF.Ln, bias=cp(3)),
                              deps=[t_var], inc=True)
                ph2[("lv", tc2)] = t_lv
                t_rs = S.emit("scalar",
                              lambda s, sbase=sbase:
                              s.activation(small2[:, sbase + 1:sbase + 2],
                                           small2[:, sbase + 4:sbase + 5],
                                           AF.Exp, scale=cp(4)),
                              deps=[t_lv, ph2.get(("x2", tc2))], inc=True)
                t_dbg = None
                if debug:
                    t_dbg = S.emit("gpsimd",
                                   lambda g, tc2=tc2, tt=tt, tc=tc:
                                   g.dma_start(dbg_y2[tt * 4 + tc, :, :], y2t[:, tc2, :]),
                                   deps=[ph2["y2e"][(tc2, hh)] for hh in range(4)],
                                   inc="gq")
                t_x2 = S.emit("vector",
                              lambda v, tc2=tc2, sbase=sbase:
                              v.tensor_scalar(y2t[:, tc2, :], y2t[:, tc2, :],
                                              small2[:, sbase:sbase + 1],
                                              small2[:, sbase + 1:sbase + 2],
                                              ALU.add, ALU.mult),
                              deps=[t_ng, t_rs, t_dbg], inc=True)
                ph2[("x2", tc2)] = t_x2
                r0 = tt * 512 + tc * 128

                def mk_out(r0, tc2, t_x2, tcb):
                    def emit_out():
                        t_out = S.emit("gpsimd",
                                       lambda g, r0=r0, tc2=tc2:
                                       g.dma_start(out_d[r0:r0 + 128, :], y2t[:, tc2, :]),
                                       deps=[t_x2], inc="out%d" % tc2)
                        ph2["outt"][(tcb, tc2)] = t_out
                        t_out_toks.append(t_out)
                    return emit_out
                ph2.setdefault("pend_out", []).append(mk_out(r0, tc2, t_x2, tcb))
        ph2["dn_done_prev"] = ("tensor", S.cnt["tensor"])
        if tt == 3:
            for f in ph2.pop("pend_out", []):
                f()

    if debug:
        S.emit("gpsimd", lambda g: g.dma_start(dbg_fuse[...], fuse[...]),
               deps=[ph2["fuse"][63]], inc="gq")

    import contextlib
    with contextlib.ExitStack() as stack:
        sems = {}
        for nm in ("tensor", "scalar", "vector") + DMA_SEMS:
            sems[nm] = stack.enter_context(nc.semaphore("sem_" + nm))
        sems["gpsimd"] = sems["gq"]
        sems["sync"] = sems["sq"]
        n_out0 = S.cnt.get("out0", 0)
        n_out1 = S.cnt.get("out1", 0)
        with nc.Block() as block:

            @block.tensor
            def _(t):
                S.replay("tensor", t, sems, {})

            @block.scalar
            def _(s):
                S.replay("scalar", s, sems, {})

            @block.vector
            def _(v):
                S.replay("vector", v, sems, {})

            @block.gpsimd
            def _(g):
                S.replay("gpsimd", g, sems, {})

            @block.sync
            def _(s):
                S.replay("sync", s, sems, {})
                s.wait_ge(sems["out0"], n_out0)
                s.wait_ge(sems["out1"], n_out1)

    return nc


_CACHE = {}


def kernel(L0_attn_X, L0_attn_Wq, L0_attn_Wk, L0_attn_Wv, L0_attn_Wo,
           L0_ffn_w_gate, L0_ffn_w_up, L0_ffn_w_down,
           L0_ffn_s_gate, L0_ffn_s_up, L0_ffn_s_down):
    X = np.asarray(L0_attn_X, dtype=np.float32)
    B = X.shape[0]
    if "nc" not in _CACHE:
        _CACHE["nc"] = build_nc()
    nc = _CACHE["nc"]
    shared = {
        "wqt": np.ascontiguousarray(np.asarray(L0_attn_Wq, dtype=np.float32).T),
        "wkt": np.ascontiguousarray(np.asarray(L0_attn_Wk, dtype=np.float32).T),
        "wvt": np.ascontiguousarray(np.asarray(L0_attn_Wv, dtype=np.float32).T),
        "wot": np.ascontiguousarray(np.asarray(L0_attn_Wo, dtype=np.float32).T),
        "wg": np.ascontiguousarray(np.asarray(L0_ffn_w_gate, dtype=np.int8)),
        "wu": np.ascontiguousarray(np.asarray(L0_ffn_w_up, dtype=np.int8)),
        "wdn": np.ascontiguousarray(np.asarray(L0_ffn_w_down, dtype=np.int8)),
        "sgp": np.ascontiguousarray(
            np.asarray(L0_ffn_s_gate, dtype=np.float32).reshape(64, 128).T),
        "sup": np.ascontiguousarray(
            np.asarray(L0_ffn_s_up, dtype=np.float32).reshape(64, 128).T),
        "sdp": np.ascontiguousarray(
            np.asarray(L0_ffn_s_down, dtype=np.float32).reshape(16, 128).T),
        "sdr": np.ascontiguousarray(
            np.asarray(L0_ffn_s_down, dtype=np.float32).reshape(1, 2048)),
    }
    in_maps = []
    for c in range(B):
        m = dict(shared)
        m["xfm"] = np.ascontiguousarray(X[c].T)
        in_maps.append(m)
    res = run_bass_kernel_spmd(nc, in_maps, core_ids=list(range(B)))
    return np.stack([res.results[c]["out"] for c in range(B)], axis=0)


# version 26 (final)
# speedup vs baseline: 1.1869x; 1.1869x over previous
import numpy as np
import ml_dtypes
import concourse.bass as bass
import concourse.mybir as mybir
from concourse.bass_utils import run_bass_kernel_spmd

F32 = mybir.dt.float32
BF16 = mybir.dt.bfloat16
I8 = mybir.dt.int8
AF = mybir.ActivationFunctionType
ALU = mybir.AluOpType
F32R = mybir.dt.float32r

T = 2048
D = 2048
NH = 16
DH = 128
IO = 8192
NJC = 16
QT = 512
NQT = T // QT
NIC = IO // 128
MSHIFT = -32.0
SCL = float(DH) ** -0.5
NEG = -1.0e30

DMA_SEMS = ("gq", "sq", "sxx", "wq", "wk", "wv", "wo0", "wo1", "xt0", "xt1",
            "gw0", "gw1", "uw0", "uw1", "dq0", "dq1", "dq2", "dq3", "out0", "out1")
ENGINES = ("tensor", "scalar", "vector", "gpsimd", "sync")


class Sched:
    def __init__(self, strict=False):
        self.ops = {e: [] for e in ENGINES}
        self.cnt = {}
        self.strict = strict

    def emit(self, eng, fn, deps=(), inc=None):
        deps2 = []
        for d in deps:
            if d is None:
                continue
            s, v = d
            if not self.strict and s == eng:
                continue
            if (s, eng) in (("gq", "gpsimd"), ("sq", "sync")):
                continue
            deps2.append((s, v))
        if self.strict and eng not in DMA_SEMS:
            if self.cnt.get(eng, 0) > 0:
                deps2.append((eng, self.cnt[eng]))
            if not inc:
                inc = True
        tok = None
        if inc:
            sem = eng if inc is True else inc
            a = 16 if sem in DMA_SEMS else 1
            self.cnt[sem] = self.cnt.get(sem, 0) + a
            tok = (sem, self.cnt[sem])
        self.ops[eng].append((deps2, fn, tok))
        return tok

    def replay(self, eng_name, eng, sems, wm):
        for deps, fn, tok in self.ops[eng_name]:
            coll = {}
            for s, v in deps:
                if v > coll.get(s, 0):
                    coll[s] = v
            for s, v in coll.items():
                if wm.get(s, 0) < v:
                    eng.wait_ge(sems[s], v)
                    wm[s] = v
            ins = fn(eng)
            if tok is not None:
                s, v = tok
                ins.then_inc(sems[s], 16 if s in DMA_SEMS else 1)


def build_nc(debug=False):
    import os
    strict = os.environ.get("BASS_SIM_FP64") == "1"
    nc = bass.Bass("TRN2", target_bir_lowering=False)

    xfm = nc.dram_tensor("xfm", [D, T], F32, kind="ExternalInput").ap()
    wqt = nc.dram_tensor("wqt", [D, D], F32, kind="ExternalInput").ap()
    wkt = nc.dram_tensor("wkt", [D, D], F32, kind="ExternalInput").ap()
    wvt = nc.dram_tensor("wvt", [D, D], F32, kind="ExternalInput").ap()
    wot = nc.dram_tensor("wot", [D, D], F32, kind="ExternalInput").ap()
    wg = nc.dram_tensor("wg", [D, IO], I8, kind="ExternalInput").ap()
    wu = nc.dram_tensor("wu", [D, IO], I8, kind="ExternalInput").ap()
    wdn = nc.dram_tensor("wdn", [IO, D], I8, kind="ExternalInput").ap()
    sgp_d = nc.dram_tensor("sgp", [128, 64], F32, kind="ExternalInput").ap()
    sup_d = nc.dram_tensor("sup", [128, 64], F32, kind="ExternalInput").ap()
    sdp_d = nc.dram_tensor("sdp", [128, 16], F32, kind="ExternalInput").ap()
    sdr_d = nc.dram_tensor("sdr", [1, D], F32, kind="ExternalInput").ap()
    out_d = nc.dram_tensor("out", [T, D], F32, kind="ExternalOutput").ap()
    wdnb = nc.dram_tensor("wdnb", [IO, D], BF16, kind="Internal").ap()

    th = 10000.0 ** (-2.0 * np.arange(DH // 2, dtype=np.float64) / DH)
    ang = np.arange(T, dtype=np.float64)[:, None] * th[None, :]
    cosv = np.cos(ang).astype(np.float32).T
    sinv = np.sin(ang).astype(np.float32).T
    cosF = np.empty((128, T), dtype=np.float32)
    sinF = np.empty((128, T), dtype=np.float32)
    cosF[0::2, :] = cosv
    cosF[1::2, :] = cosv
    sinF[0::2, :] = -sinv
    sinF[1::2, :] = sinv
    pswap = np.zeros((128, 128), dtype=np.float32)
    for i in range(64):
        pswap[2 * i + 1, 2 * i] = 1.0
        pswap[2 * i, 2 * i + 1] = 1.0
    tri = np.zeros((128, 128), dtype=np.float32)
    for p in range(128):
        tri[p, :p] = NEG
    ident = np.eye(128, dtype=np.float32)
    onescol = np.ones((128, 1), dtype=np.float32)
    onesrow = np.ones((1, 128), dtype=np.float32)
    cpool_v = np.array([MSHIFT, SCL, 1.0 / D, 1e-5, -0.5, -1.0, -1.0 / D, 0.0],
                       dtype=np.float32)[None, :].repeat(128, axis=0)

    cosF_d = nc.inline_tensor(np.ascontiguousarray(cosF), "cosFc").ap()
    sinF_d = nc.inline_tensor(np.ascontiguousarray(sinF), "sinFc").ap()
    pswap_d = nc.inline_tensor(pswap, "pswapc").ap()
    tri_d = nc.inline_tensor(tri, "tric").ap()
    ident_d = nc.inline_tensor(ident, "identc").ap()
    onescol_d = nc.inline_tensor(onescol, "onescolc").ap()
    onesrow_d = nc.inline_tensor(onesrow, "onesrowc").ap()
    cpool_d = nc.inline_tensor(np.ascontiguousarray(cpool_v), "cpoolc").ap()

    BASE = 16512
    R_A = BASE
    R_B = BASE + 65536
    R_W = BASE + 131072
    R_K = BASE + 196608
    at = nc.alloc_sbuf_tensor_at

    xbf = at("xbf", [128, NJC, T], BF16, offset=R_A)
    heads = at("heads", [128, NJC, T], BF16, offset=R_B)
    fuse = at("fuse", [128, NIC, QT], BF16, offset=R_B)

    cosf = at("cosf", [128, T], BF16, offset=R_W + 0)
    sinf = at("sinf", [128, T], BF16, offset=R_W + 4096)
    qraw = at("qraw", [128, T], BF16, offset=R_W + 8192)
    qr = at("qr", [128, 2, T], BF16, offset=R_W + 12288)
    kr = at("kr", [128, 2, T], BF16, offset=R_W + 20480)
    pt = at("pt", [128, 4, QT], BF16, offset=R_W + 28672)
    vtm = at("vtm", [128, 16, 256], BF16, offset=R_W + 32768)
    wqs = at("wqs", [128, NJC, 128], BF16, offset=R_W + 40960)
    wks = at("wks", [128, NJC, 128], BF16, offset=R_W + 45056)
    wvs = at("wvs", [128, NJC, 256], BF16, offset=R_W + 49152)
    rsc1 = at("rsc1", [128, 512], F32, offset=R_W + 57344)
    rsc2 = at("rsc2", [128, 512], F32, offset=R_W + 59392)
    sdrow = at("sdrow", [1, D], F32, offset=R_W + 57344)

    wo_init = at("wo_init", [128, 2, NJC, 128], BF16, offset=R_W + 0)
    ybf = at("ybf", [128, NJC, 512], BF16, offset=R_W + 0)
    ysq = at("ysq", [128, 512], BF16, offset=R_W + 16384)
    wos = at("wos", [128, 2, NJC, 128], BF16, offset=R_W + 17408)
    xtile = at("xtile", [128, 2, 512], F32, offset=R_W + 25600)
    m_sb = at("m_sb", [128, 512], BF16, offset=R_W + 29696)
    r_sb = at("r_sb", [128, 512], BF16, offset=R_W + 30720)

    dwn = at("dwn", [128, 4, 4, 512], BF16, offset=R_W + 0)
    y2t = at("y2t", [128, 2, D], BF16, offset=R_W + 16384)
    gws = at("gws", [128, 2, NJC, 256], BF16, offset=R_W + 24576)
    uws = at("uws", [128, 2, NJC, 256], BF16, offset=R_W + 40960)
    diag = at("diag", [128, 16, 128], BF16, offset=R_W + 57344)
    gsc = at("gsc", [128, 512], F32, offset=R_W + 61440)
    usc = at("usc", [128, 512], F32, offset=R_W + 63488)

    s_bc = at("s_bc", [128, D], F32, offset=R_K + 0)
    pswap_sb = at("pswap_sb", [128, 128], BF16, offset=R_K + 8192)
    tri_sb = at("tri_sb", [128, 128], BF16, offset=R_K + 8448)
    ident_sb = at("ident_sb", [128, 128], BF16, offset=R_K + 8704)
    onescol_sb = at("onescol_sb", [128, 1], BF16, offset=R_K + 8960)
    onesrow_sb = at("onesrow_sb", [1, 128], F32, offset=R_K + 8992)
    cpool_sb = at("cpool_sb", [128, 8], F32, offset=R_K + 9504)
    sg_sb = at("sg_sb", [128, 64], F32, offset=R_K + 9536)
    su_sb = at("su_sb", [128, 64], F32, offset=R_K + 9792)
    rowbuf = at("rowbuf", [1, QT], F32, offset=R_K + 10048)
    rowbuf2 = at("rowbuf2", [1, QT], F32, offset=R_K + 12096)
    recipbc = at("recipbc", [128, QT], F32, offset=R_K + 14144)
    bnsc = at("bnsc", [128, 24], F32, offset=R_K + 14144)
    bnmv = at("bnmv", [128, 4], F32, offset=R_K + 14240)
    small2 = at("small2", [128, 16], F32, offset=R_K + 14272)
    sdpp = at("sdpp", [128, 16], F32, offset=R_K + 14336)
    rsdp = at("rsdp", [128, 16], F32, offset=R_K + 14400)
    assert R_K + 16192 <= 229344

    PSD01 = nc.alloc_psum_tensor("psd01", [128, 1024], F32)
    PSD23 = nc.alloc_psum_tensor("psd23", [128, 1024], F32)
    PS = [PSD01[:, 0:512], PSD01[:, 512:1024],
          PSD23[:, 0:512], PSD23[:, 512:1024]] + [
        nc.alloc_psum_tensor("psb%d" % i, [128, 512], F32)[...] for i in range(4, 8)]
    PSDP = [PSD01, PSD23]

    if debug:
        dbg_x1 = nc.dram_tensor("dbg_x1", [128, NJC, T], F32, kind="ExternalOutput").ap()
        dbg_heads = nc.dram_tensor("dbg_heads", [128, NJC, T], F32, kind="ExternalOutput").ap()
        dbg_fuse = nc.dram_tensor("dbg_fuse", [128, NIC, QT], F32, kind="ExternalOutput").ap()
        dbg_y2 = nc.dram_tensor("dbg_y2", [16, 128, D], F32, kind="ExternalOutput").ap()

    S = Sched(strict=strict)

    def cp(idx, p=128):
        return cpool_sb[0:p, idx:idx + 1]

    xfm_r = xfm.rearrange("(jc p) t -> p jc t", p=128)
    wqt_r = wqt.rearrange("(jc p) i -> p jc i", p=128)
    wkt_r = wkt.rearrange("(jc p) i -> p jc i", p=128)
    wvt_r = wvt.rearrange("(jc p) i -> p jc i", p=128)
    wot_r = wot.rearrange("(jc p) i -> p jc i", p=128)
    wg_r = wg.rearrange("(hc p) io -> p hc io", p=128)
    wu_r = wu.rearrange("(hc p) io -> p hc io", p=128)
    wdn_r = wdn.rearrange("(icb ic p) ho -> p icb ic ho", p=128, ic=4)
    wdnb_r = wdnb.rearrange("(icb ic p) ho -> p icb ic ho", p=128, ic=4)

    t_cn = {}
    for nm, dst, src in (("cos", cosf, cosF_d), ("sin", sinf, sinF_d),
                         ("psw", pswap_sb, pswap_d), ("tri", tri_sb, tri_d),
                         ("idn", ident_sb, ident_d), ("onc", onescol_sb, onescol_d)):
        t_cn[nm] = S.emit("gpsimd", lambda g, a=dst, b=src: g.dma_start(a[...], b), inc="gq")
    for nm in ("cos", "sin", "psw", "tri", "idn", "onc"):
        t_cn[nm] = t_cn["onc"]
    for nm, dst, src in (("onr", onesrow_sb, onesrow_d), ("cpl", cpool_sb, cpool_d),
                         ("sg", sg_sb, sgp_d), ("su", su_sb, sup_d), ("sdr", sdrow, sdr_d)):
        t_cn[nm] = S.emit("sync", lambda s, a=dst, b=src: s.dma_start(a[...], b), inc="sq")
    for nm in ("onr", "cpl", "sg", "su", "sdr"):
        t_cn[nm] = t_cn["sdr"]

    t_x = []
    for j4 in range(4):
        t_x.append(S.emit("gpsimd",
                          lambda g, j4=j4: g.dma_start(xbf[:, j4 * 4:(j4 + 1) * 4, :],
                                                       xfm_r[:, j4 * 4:(j4 + 1) * 4, :]),
                          inc="sxx"))
    t_xc = t_x[-1]

    t_sbc = None
    for n4 in range(4):
        tm = S.emit("tensor",
                    lambda t, n4=n4: t.matmul(PS[6], onesrow_sb[:, :],
                                              sdrow[0:1, n4 * 512:(n4 + 1) * 512],
                                              start=True, stop=True, skip_group_check=True),
                    deps=[t_cn["onr"], t_cn["sdr"], t_sbc], inc=True)
        t_sbc = S.emit("scalar",
                       lambda s, n4=n4: s.copy(s_bc[:, n4 * 512:(n4 + 1) * 512], PS[6][:, :]),
                       deps=[tm], inc=True)

    def load_wslice(dst_ap, src_r, c0, c1, deps, sem="gq"):
        tk = None
        for h2 in range(2):
            tk = S.emit("gpsimd",
                        lambda g, h2=h2, dst_ap=dst_ap, src_r=src_r, c0=c0, c1=c1:
                        g.dma_start(dst_ap[:, h2 * 8:(h2 + 1) * 8, :],
                                    src_r[:, h2 * 8:(h2 + 1) * 8, c0:c1]),
                        deps=deps if h2 == 0 else [], inc=sem)
        return tk

    st = {
        "projbank": None,
        "ps5_rd": None,
        "rope": {},
        "lastS": {},
        "wq": {}, "wk": {}, "wv": {},
        "qk_mm_done": {},
        "v_mm_done": {},
        "vtm_ev": {},
        "exp_bank": {},
        "pt_rd": {},
        "sums_free": None,
        "rb2_rd": None,
        "ps6_rd": None,
        "recip_rd": None,
        "pv_free": None,
        "qraw_rd": None,
        "hevac": {},
        "rsc_rd": None,
    }

    st["wq"][0] = load_wslice(wqs, wqt_r, 0, 128, [t_cn["sdr"]], sem="wq")
    st["wk"][0] = load_wslice(wks, wkt_r, 0, 128, [], sem="wk")
    st["wv"][0] = load_wslice(wvs, wvt_r, 0, 256, [t_sbc], sem="wv")

    def emit_qk_proj(h, which):
        par = h % 2
        dst = qr if which == "q" else kr
        w_sb = wqs if which == "q" else wks
        t_w = st["wq" if which == "q" else "wk"][h]
        last_mm = None
        for tt in range(4):
            sl = slice(tt * 512, (tt + 1) * 512)
            tmm = None
            for jc in range(NJC):
                d = []
                if jc == 0:
                    d = [t_w, st["projbank"], t_xc]
                tmm = S.emit("tensor",
                             lambda t, jc=jc, sl=sl, w_sb=w_sb:
                             t.matmul(PS[7][:, :], w_sb[:, jc, :], xbf[:, jc, sl],
                                      start=(jc == 0), stop=(jc == NJC - 1),
                                      skip_group_check=True),
                             deps=d, inc=(jc == NJC - 1))
            last_mm = tmm
            tev = S.emit("vector",
                         lambda v, sl=sl: v.tensor_copy(qraw[:, sl], PS[7]),
                         deps=[tmm, st["qraw_rd"]], inc=True)
            st["projbank"] = tev
            tqs = S.emit("tensor",
                         lambda t, sl=sl: t.matmul(PS[3][:, :], pswap_sb[:, :], qraw[:, sl],
                                                   start=True, stop=True,
                                                   skip_group_check=True),
                         deps=[tev, st["ps5_rd"], st["exp_bank"].get(1),
                               t_cn["psw"]], inc=True)
            ta = S.emit("vector",
                        lambda v, sl=sl: v.tensor_tensor(rsc1[:, :], qraw[:, sl], cosf[:, sl],
                                                         ALU.mult),
                        deps=[tev, t_cn["cos"], st["rsc_rd"], t_sbc], inc=True)
            tb = S.emit("vector",
                        lambda v, sl=sl: v.tensor_tensor(rsc2[:, :], PS[3][:, :], sinf[:, sl],
                                                         ALU.mult),
                        deps=[tqs, t_cn["sin"]], inc=True)
            st["ps5_rd"] = tb
            tc_ = S.emit("vector",
                         lambda v, sl=sl, dst=dst, par=par:
                         v.tensor_tensor(dst[:, par, sl], rsc1[:, :], rsc2[:, :], ALU.add),
                         deps=[st["lastS"].get(h - 2)], inc=True)
            st["rsc_rd"] = tc_
            st["qraw_rd"] = tc_
            st["rope"][(h, which, tt)] = tc_
        st["qk_mm_done"][h] = ("tensor", S.cnt["tensor"])
        return last_mm

    def emit_prep(h):
        emit_qk_proj(h, "q")
        emit_qk_proj(h, "k")
        if h + 1 < NH:
            st["wq"][h + 1] = load_wslice(wqs, wqt_r, (h + 1) * 128, (h + 2) * 128,
                                          [st["qk_mm_done"][h]], sem="wq")
            st["wk"][h + 1] = load_wslice(wks, wkt_r, (h + 1) * 128, (h + 2) * 128,
                                          [st["qk_mm_done"][h]], sem="wk")


    def emit_vproj(g):
        for tcc in range(16):
            tmm = None
            for jc in range(NJC):
                d = []
                if jc == 0:
                    d = [st["wv"][g], t_xc, st["projbank"]]
                tmm = S.emit("tensor",
                             lambda t, jc=jc, tcc=tcc:
                             t.matmul(PS[7][:, 0:256], xbf[:, jc, tcc * 128:(tcc + 1) * 128],
                                      wvs[:, jc, :],
                                      start=(jc == 0), stop=(jc == NJC - 1),
                                      skip_group_check=True),
                             deps=d, inc=(jc == NJC - 1))
            tev = S.emit("vector",
                         lambda v, tcc=tcc: v.tensor_copy(vtm[:, tcc, :], PS[7][:, 0:256]),
                         deps=[tmm], inc=True)
            st["projbank"] = tev
        st["v_mm_done"][g] = ("tensor", S.cnt["tensor"])
        st["vtm_ev"][g] = ("vector", S.cnt["vector"])
        if g + 1 < 8:
            st["wv"][g + 1] = load_wslice(wvs, wvt_r, (g + 1) * 256, (g + 2) * 256,
                                          [st["v_mm_done"][g]], sem="wv")

    def emit_attn(h):
        par = h % 2
        g = h // 2
        off = (h % 2) * 128
        for qt in range(NQT):
            nchunks = 4 * qt + 4
            qsl = slice(qt * 512, (qt + 1) * 512)
            sums_ps = PS[5]
            t_S = {}
            t_tri = {}
            t_sum_last = None
            t_pv_last = None
            pending = {}

            def emit_S_pair(cp_i):
                cc = [2 * cp_i, 2 * cp_i + 1]
                slots = [c % 4 for c in cc]
                pp = cp_i % 2
                for c in cc:
                    bank = 2 * pp + (c % 2)
                    d = [st["rope"][(h, "k", c // 4)], st["exp_bank"].get(pp)]
                    if pp == 1 and c % 2 == 0:
                        d.append(st["ps5_rd"])
                    if c == 0:
                        d += [st["rope"][(h, "q", qt)], st["vtm_ev"][g]]
                    t_S[c] = S.emit("tensor",
                                    lambda t, c=c, bank=bank, par=par, qsl=qsl:
                                    t.matmul(PS[bank],
                                             kr[:, par, c * 128:(c + 1) * 128],
                                             qr[:, par, qsl],
                                             start=True, stop=True, skip_group_check=True),
                                    deps=d, inc=True)
                for c in cc:
                    if c >= 4 * qt:
                        bank = 2 * pp + (c % 2)
                        c0 = (c - 4 * qt) * 128
                        t_tri[c] = S.emit("vector",
                                          lambda v, c=c, bank=bank, c0=c0:
                                          v.tensor_tensor(PS[bank][:, c0:c0 + 128],
                                                          PS[bank][:, c0:c0 + 128],
                                                          tri_sb[:, :], ALU.add),
                                          deps=[t_S[c], t_cn["tri"]], inc=True)
                t_exp = {}
                t_ms = {}
                for c in cc:
                    bank = 2 * pp + (c % 2)
                    slot = c % 4
                    d = [t_S[c], t_tri.get(c), st["pt_rd"].get(slot), t_cn["cpl"]]
                    t_exp[c] = S.emit("scalar",
                                      lambda s, slot=slot, bank=bank:
                                      s.activation(pt[:, slot, :], PS[bank],
                                                   AF.Exp, bias=cp(0), scale=cp(1)),
                                      deps=d, inc=True)
                    c0 = (c - 4 * qt) * 128
                    if c >= 4 * qt and c0 > 0:
                        t_ms[c] = S.emit("vector",
                                         lambda v, slot=slot, c0=c0:
                                         v.memset(pt[:, slot, 0:c0], 0.0),
                                         deps=[t_exp[c]], inc=True)
                st["exp_bank"][pp] = t_exp[cc[1]]
                pending[cp_i] = (cc, t_exp, t_ms)

            def emit_consume_pair(cp_i):
                nonlocal t_sum_last, t_pv_last
                cc, t_exp, t_ms = pending.pop(cp_i)
                for c in cc:
                    slot = c % 4
                    d = [t_exp[c], t_ms.get(c)]
                    if c == 0:
                        d.append(st["sums_free"])
                    t_sum_last = S.emit("tensor",
                                        lambda t, c=c, slot=slot, sums_ps=sums_ps,
                                        nchunks=nchunks:
                                        t.matmul(sums_ps[0:1, :], onescol_sb[:, :],
                                                 pt[:, slot, :],
                                                 start=(c == 0), stop=(c == nchunks - 1),
                                                 skip_group_check=True),
                                        deps=d + [t_cn["onc"]], inc=(c == nchunks - 1))
                    d = []
                    if c == 0:
                        d = [st["pv_free"]]
                    t_pv_last = S.emit("tensor",
                                       lambda t, c=c, slot=slot, off=off, nchunks=nchunks:
                                       t.matmul(PS[4], vtm[:, c, off:off + 128],
                                                pt[:, slot, :],
                                                start=(c == 0), stop=(c == nchunks - 1),
                                                skip_group_check=True),
                                       deps=d, inc=True)
                    st["pt_rd"][slot] = t_pv_last

            for cp_i in range(nchunks // 2):
                emit_S_pair(cp_i)
                if cp_i == 0 and st.get("pend_chain"):
                    st.pop("pend_chain")()
                if cp_i > 0:
                    emit_consume_pair(cp_i - 1)
            emit_consume_pair(nchunks // 2 - 1)
            st["lastS"][h] = ("tensor", S.cnt["tensor"])
            t_ln = S.emit("scalar",
                          lambda s, sums_ps=sums_ps: s.activation(rowbuf[0:1, :],
                                                                  sums_ps[0:1, :], AF.Ln),
                          deps=[t_sum_last], inc=True)
            st["sums_free"] = t_ln
            t_rc = S.emit("scalar",
                          lambda s: s.activation(rowbuf2[0:1, :], rowbuf[0:1, :],
                                                 AF.Exp, scale=cp(5, 1)),
                          deps=[st["rb2_rd"]], inc=True)

            def mk_chain(h, qt, qsl, t_rc, t_pv_last):
                def emit_chain():
                    t_bc = S.emit("tensor",
                                  lambda t: t.matmul(PS[6], onesrow_sb[:, :],
                                                     rowbuf2[0:1, :],
                                                     start=True, stop=True,
                                                     skip_group_check=True),
                                  deps=[t_rc, st["ps6_rd"]], inc=True)
                    st["rb2_rd"] = t_bc
                    t_cpy = S.emit("vector",
                                   lambda v: v.tensor_copy(recipbc[:, :], PS[6]),
                                   deps=[t_bc, st["recip_rd"]], inc=True)
                    st["ps6_rd"] = t_cpy
                    t_hv = S.emit("vector",
                                  lambda v, h=h, qsl=qsl:
                                  v.tensor_tensor(heads[:, h, qsl], PS[4], recipbc[:, :],
                                                  ALU.mult),
                                  deps=[t_pv_last, t_cpy], inc=True)
                    st["recip_rd"] = t_hv
                    st["pv_free"] = t_hv
                    st["hevac"][(h, qt)] = t_hv
                return emit_chain
            st["pend_chain"] = mk_chain(h, qt, qsl, t_rc, t_pv_last)

    st["ps6_rd"] = t_sbc
    emit_prep(0)
    emit_vproj(0)
    for h in range(NH):
        if h + 1 < NH:
            emit_prep(h + 1)
        emit_attn(h)
        if h % 2 == 1 and h // 2 + 1 < 8:
            emit_vproj(h // 2 + 1)

    if st.get("pend_chain"):
        st.pop("pend_chain")()
    t_attn = st["hevac"][(NH - 1, NQT - 1)]

    t_wo_ld = {}
    t_wo_ld[0] = load_wslice(wo_init[:, 0], wot_r, 0, 128,
                             [st["rope"][(NH - 1, "k", 3)]], sem="wo0")
    t_wo_ld[1] = load_wslice(wo_init[:, 1], wot_r, 128, 256, [], sem="wo1")
    t_xt = {}
    t_womm = {}
    t_yadd = {}
    t_stats = {}
    ln1 = {"ps6_rd": st["ps6_rd"], "ps7_rd": st["projbank"], "x1": {},
           "m_rd": None, "r_rd": st["rb2_rd"], "mcp": None, "rcp": None,
           "pending": None}

    for tt in range(4):
        tsl = slice(tt * 512, (tt + 1) * 512)
        t_cs = None
        t_sq = None
        for ic in range(NJC):
            n = tt * 16 + ic
            par = ic % 2
            bank = 4 + par
            if ln1["pending"] is not None:
                ln1["pending"](ic)
            t_xt[(tt, ic)] = S.emit("sync",
                                    lambda s, ic=ic, par=par, tsl=tsl:
                                    s.dma_start(xtile[:, par, :], xfm_r[:, ic, tsl]),
                                    deps=[t_attn if n < 2 else None,
                                          t_yadd.get((tt, ic - 2)) or
                                          (t_yadd.get((tt - 1, 14 + ic)) if tt > 0 else None)],
                                    inc="xt%d" % par)
            tmm = None
            for jc in range(NJC):
                d = []
                if jc == 0:
                    d = [t_wo_ld[n], t_attn, st["sums_free"],
                         t_yadd.get((tt, ic - 2)) or
                         (t_yadd.get((tt - 1, 14 + ic)) if tt > 0 else None)]
                wsrc = wo_init if n < 2 else wos
                tmm = S.emit("tensor",
                             lambda t, jc=jc, ic=ic, bank=bank, tsl=tsl, wsrc=wsrc:
                             t.matmul(PS[bank][:, :], wsrc[:, ic % 2, jc, :],
                                      heads[:, jc, tsl],
                                      start=(jc == 0), stop=(jc == NJC - 1),
                                      skip_group_check=True),
                             deps=d, inc=(jc == NJC - 1))
            t_womm[(tt, ic)] = tmm
            if n + 2 < 64:
                nic = (n + 2) % 16
                t_wo_ld[n + 2] = load_wslice(wos[:, (n + 2) % 2], wot_r,
                                             nic * 128, (nic + 1) * 128, [tmm],
                                             sem="wo%d" % ((n + 2) % 2))
            t_yadd[(tt, ic)] = S.emit("vector",
                                      lambda v, ic=ic, par=par, bank=bank:
                                      v.tensor_tensor(ybf[:, ic, :], PS[bank][:, :],
                                                      xtile[:, par, :], ALU.add),
                                      deps=[tmm, t_xt[(tt, ic)]],
                                      inc=True)
            t_ysq = S.emit("vector",
                           lambda v, ic=ic: v.tensor_tensor(ysq[:, :], ybf[:, ic, :],
                                                            ybf[:, ic, :], ALU.mult),
                           deps=[t_stats.get("sq")], inc=True)
            t_cs = S.emit("tensor",
                          lambda t, ic=ic: t.matmul(PS[6][0:1, :], onescol_sb[:, :],
                                                    ybf[:, ic, :],
                                                    start=(ic == 0), stop=(ic == NJC - 1),
                                                    skip_group_check=True),
                          deps=[t_yadd[(tt, ic)]] + ([ln1["ps6_rd"]] if ic == 0 else []),
                          inc=(ic == NJC - 1))
            t_sq = S.emit("tensor",
                          lambda t, ic=ic: t.matmul(PS[7][0:1, :], onescol_sb[:, :],
                                                    ysq[:, :],
                                                    start=(ic == 0), stop=(ic == NJC - 1),
                                                    skip_group_check=True),
                          deps=[t_ysq] + ([ln1["ps7_rd"]] if ic == 0 else []),
                          inc=(ic == NJC - 1))
            t_stats["sq"] = t_sq
        t_m = S.emit("scalar",
                     lambda s: s.activation(rowbuf[0:1, :], PS[6][0:1, :], AF.Copy,
                                            scale=cp(2, 1)),
                     deps=[t_cs, ln1["m_rd"]], inc=True)
        t_msq = S.emit("scalar",
                       lambda s: s.activation(rowbuf2[0:1, :], PS[7][0:1, :], AF.Copy,
                                              scale=cp(2, 1)),
                       deps=[t_sq, ln1["r_rd"]], inc=True)
        t_m2 = S.emit("vector",
                      lambda v: v.tensor_tensor(recipbc[0:1, :], rowbuf[0:1, :],
                                                rowbuf[0:1, :], ALU.mult),
                      deps=[t_m, st["recip_rd"]], inc=True)
        t_var = S.emit("vector",
                       lambda v: v.tensor_tensor(rowbuf2[0:1, :], rowbuf2[0:1, :],
                                                 recipbc[0:1, :], ALU.subtract),
                       deps=[t_msq], inc=True)
        t_lnv = S.emit("scalar",
                       lambda s: s.activation(recipbc[0:1, :], rowbuf2[0:1, :], AF.Ln,
                                              bias=cp(3, 1)),
                       deps=[t_var, t_m2], inc=True)
        t_rstd = S.emit("scalar",
                        lambda s: s.activation(rowbuf2[0:1, :], recipbc[0:1, :], AF.Exp,
                                               scale=cp(4, 1)),
                        deps=[t_lnv], inc=True)
        t_mbc = S.emit("tensor",
                       lambda t: t.matmul(PS[2], onesrow_sb[:, :], rowbuf[0:1, :],
                                          start=True, stop=True, skip_group_check=True),
                       deps=[t_m, t_m2,
                             (t_attn if tt == 0 else ln1["mcp"])],
                       inc=True)
        ln1["m_rd"] = t_mbc
        t_rbc = S.emit("tensor",
                       lambda t: t.matmul(PS[3], onesrow_sb[:, :], rowbuf2[0:1, :],
                                          start=True, stop=True, skip_group_check=True),
                       deps=[t_rstd,
                             (t_attn if tt == 0 else ln1["rcp"])],
                       inc=True)
        ln1["r_rd"] = t_rbc
        st["recip_rd"] = t_lnv
        t_mcp = S.emit("vector", lambda v: v.tensor_copy(m_sb[:, :], PS[2]),
                       deps=[t_mbc], inc=True)
        ln1["mcp"] = t_mcp
        t_rcp = S.emit("vector", lambda v: v.tensor_copy(r_sb[:, :], PS[3]),
                       deps=[t_rbc], inc=True)
        ln1["rcp"] = t_rcp

        def mk_x1(tt, tsl):
            def emit_x1(ic):
                S.emit("vector",
                       lambda v, ic=ic: v.tensor_tensor(xbf[:, ic, tsl], ybf[:, ic, :],
                                                        m_sb[:, :], ALU.subtract),
                       deps=[], inc=True)
                ln1["x1"][(tt, ic)] = S.emit(
                    "vector",
                    lambda v, ic=ic: v.tensor_tensor(xbf[:, ic, tsl], xbf[:, ic, tsl],
                                                     r_sb[:, :], ALU.mult),
                    deps=[], inc=True)
            return emit_x1
        ln1["pending"] = mk_x1(tt, tsl)
        if tt == 3:
            for ic in range(NJC):
                ln1["pending"](ic)
        ln1["ps6_rd"] = t_m
        ln1["ps7_rd"] = t_msq
    t_ln1_done = ln1["x1"][(3, 15)]
    t_wo_mm_all = ("tensor", S.cnt["tensor"])

    if debug:
        S.emit("gpsimd", lambda g: g.dma_start(dbg_heads[...], heads[...]),
               deps=[t_attn], inc="gq")
        S.emit("gpsimd", lambda g: g.dma_start(dbg_x1[...], xbf[...]),
               deps=[t_ln1_done], inc="gq")

    t_sdp = S.emit("sync", lambda s: s.dma_start(sdpp[...], sdp_d),
                   deps=[t_ln1_done, st["recip_rd"]], inc="sq")
    t_lnr = S.emit("scalar", lambda s: s.activation(rsdp[:, :], sdpp[:, :], AF.Ln),
                   deps=[t_sdp, t_ln1_done], inc=True)
    t_rsd = S.emit("scalar", lambda s: s.activation(sdpp[:, :], rsdp[:, :], AF.Exp,
                                                    scale=cp(5)),
                   deps=[t_lnr], inc=True)
    t_diag = None
    for dc in range(16):
        t_diag = S.emit("vector",
                        lambda v, dc=dc: v.tensor_scalar_mul(diag[:, dc, :], ident_sb[:, :],
                                                             sdpp[:, dc:dc + 1]),
                        deps=[t_rsd, t_cn["idn"], t_womm[(3, 15)], t_ln1_done], inc=True)

    ph2 = {"gw": {}, "uw": {},
           "silu": {}, "usc": {}, "fuse": {}, "dw_rd": {},
           "y2e": {}, "outt": {},
           "bank_rd": {0: t_ln1_done, 1: t_ln1_done}}

    def load_gu(which, s_idx, deps):
        src = wg_r if which == "g" else wu_r
        dst = gws if which == "g" else uws
        par = s_idx % 2
        sem = ("gw%d" if which == "g" else "uw%d") % par
        tk = None
        for h2 in range(2):
            tk = S.emit("gpsimd",
                        lambda g, h2=h2, dst=dst, src=src, s_idx=s_idx, par=par:
                        g.dma_start(dst[:, par, h2 * 8:(h2 + 1) * 8, :],
                                    src[:, h2 * 8:(h2 + 1) * 8,
                                        s_idx * 256:(s_idx + 1) * 256]),
                        deps=deps if h2 == 0 else [], inc=sem)
        return tk

    ph2["gw"][0] = load_gu("g", 0, [t_ln1_done])
    ph2["uw"][0] = load_gu("u", 0, [t_ln1_done])
    ph2["gw"][1] = load_gu("g", 1, [t_ln1_done])
    ph2["uw"][1] = load_gu("u", 1, [t_ln1_done])

    t_out_toks = []
    for tt in range(4):
        tsl = slice(tt * 512, (tt + 1) * 512)
        if tt > 0:
            for nx2, (tg_, tu_) in ph2.pop("nextgw", {}).items():
                ph2["gw"][nx2] = tg_
                ph2["uw"][nx2] = tu_
        for s_idx in range(32):
            for icw in range(2):
                ic = s_idx * 2 + icw
                tg = None
                for hc in range(NJC):
                    d = []
                    if hc == 0:
                        d = [ph2["gw"].get(s_idx) if icw == 0 else None, t_ln1_done,
                             ph2["silu"].get(ic - 1, ph2.get("last_silu"))]
                    tg = S.emit("tensor",
                                lambda t, hc=hc, ic=ic, s_idx=s_idx, icw=icw, tsl=tsl:
                                t.matmul(PS[0][:, :], gws[:, s_idx % 2, hc, icw * 128:(icw + 1) * 128],
                                         xbf[:, hc, tsl],
                                         start=(hc == 0), stop=(hc == NJC - 1),
                                         skip_group_check=True),
                                deps=d, inc=(hc == NJC - 1))
                if icw == 1:
                    ph2["g_mm_last"] = tg
                ph2["silu"][ic] = S.emit("scalar",
                                         lambda s, ic=ic:
                                         s.activation(gsc[:, :], PS[0][:, :],
                                                      AF.Silu,
                                                      scale=sg_sb[:, ic:ic + 1]),
                                         deps=[tg, ph2["fuse"].get(ic - 1), t_cn["sg"],
                                               (t_wo_mm_all if ic == 0 and tt == 0 else None)],
                                         inc=True)
                tu = None
                for hc in range(NJC):
                    d = []
                    if hc == 0:
                        d = [ph2["uw"].get(s_idx) if icw == 0 else None,
                             ph2["usc"].get(ic - 1, ph2.get("last_usc"))]
                    tu = S.emit("tensor",
                                lambda t, hc=hc, icw=icw, tsl=tsl, s_idx=s_idx:
                                t.matmul(PS[1][:, :], uws[:, s_idx % 2, hc, icw * 128:(icw + 1) * 128],
                                         xbf[:, hc, tsl],
                                         start=(hc == 0), stop=(hc == NJC - 1),
                                         skip_group_check=True),
                                deps=d, inc=(hc == NJC - 1))
                if icw == 1:
                    ph2["u_mm_last"] = tu
                ph2["usc"][ic] = S.emit("vector",
                                        lambda v, ic=ic:
                                        v.tensor_scalar_mul(usc[:, :], PS[1][:, :],
                                                            su_sb[:, ic:ic + 1]),
                                        deps=[tu, ph2["fuse"].get(ic - 1), t_cn["su"],
                                              (t_diag if ic == 0 and tt == 0 else None)],
                                        inc=True)
                ph2["fuse"][ic] = S.emit("vector",
                                         lambda v, ic=ic:
                                         v.tensor_tensor(fuse[:, ic, :], gsc[:, :],
                                                         usc[:, :], ALU.mult),
                                         deps=[ph2["silu"][ic],
                                               (t_wo_mm_all if (ic == 0 and tt == 0)
                                                else None),
                                               (ph2.get("dn_done_prev") if tt > 0
                                                else None)],
                                         inc=True)
            nxt = s_idx + 2
            if nxt < 32:
                ph2["gw"][nxt] = load_gu("g", nxt, [ph2["g_mm_last"]])
                ph2["uw"][nxt] = load_gu("u", nxt, [ph2["u_mm_last"]])
            else:
                nx2 = nxt - 32
                if tt < 3:
                    ph2["nextgw"] = ph2.get("nextgw", {})
                    ph2["nextgw"][nx2] = (load_gu("g", nx2, [ph2["g_mm_last"]]),
                                          load_gu("u", nx2, [ph2["u_mm_last"]]))
        ph2["last_silu"] = ph2["silu"][63]
        ph2["last_usc"] = ph2["usc"][63]
        t_fuse_all = ph2["fuse"][63]
        for tcb in range(2):
            for hos in range(4):
                t_dn_stop = {}
                for icb in range(16):
                    ph2["dwctr"] = ph2.get("dwctr", 0) + 1
                    par = ph2["dwctr"] % 4
                    t_dw = S.emit("gpsimd",
                                  lambda g, par=par, icb=icb, hos=hos:
                                  g.dma_start(dwn[:, par, :, :],
                                              wdn_r[:, icb, :, hos * 512:(hos + 1) * 512]),
                                  deps=[ph2["dw_rd"].get(par, t_ln1_done)],
                                  inc="dq%d" % par)
                    if icb == 12 and hos == 0 and ph2.get("pend_out"):
                        for f in ph2.pop("pend_out"):
                            f()
                    for icw in range(4):
                        ic = icb * 4 + icw
                        for tc2 in range(2):
                            tc = tcb * 2 + tc2
                            d = []
                            if icb == 0 and icw == 0:
                                d = [t_fuse_all, t_dw,
                                     ph2["bank_rd"].get(tc2)]
                            elif icw == 0:
                                d = [t_dw]
                            t_mm = S.emit("tensor",
                                          lambda t, ic=ic, icw=icw, tc=tc, tc2=tc2, par=par,
                                          icb=icb:
                                          t.matmul(PS[2 + tc2][:, :],
                                                   fuse[:, ic, tc * 128:(tc + 1) * 128],
                                                   dwn[:, par, icw, :],
                                                   start=(icb == 0 and icw == 0), stop=False,
                                                   skip_group_check=True),
                                          deps=d, inc=(icw == 3 and tc2 == 1))
                    ph2["dw_rd"][par] = t_mm
                for tc2 in range(2):
                    tc = tcb * 2 + tc2
                    for i_dc in range(4):
                        dc = hos * 4 + i_dc
                        t_dn_stop[tc2] = S.emit(
                            "tensor",
                            lambda t, tc2=tc2, tc=tc, dc=dc, i_dc=i_dc, tt=tt:
                            t.matmul(PS[2 + tc2][:, i_dc * 128:(i_dc + 1) * 128],
                                     xbf[:, dc, tt * 512 + tc * 128: tt * 512 + (tc + 1) * 128],
                                     diag[:, dc, :],
                                     start=False, stop=(i_dc == 3),
                                     skip_group_check=True),
                            deps=([t_diag, t_ln1_done] if i_dc == 0 else []),
                            inc=(i_dc == 3))
                for tc2 in range(2):
                    prev_pair = (tcb - 1, tc2) if tcb > 0 else (1, tc2)
                    ph2["y2e"][(tc2, hos)] = S.emit(
                        "vector",
                        lambda v, tc2=tc2, hos=hos:
                        v.tensor_tensor(y2t[:, tc2, hos * 512:(hos + 1) * 512],
                                        PS[2 + tc2][:, :], s_bc[:, hos * 512:(hos + 1) * 512],
                                        ALU.mult),
                        deps=[t_dn_stop[tc2],
                              (ph2["outt"].get(prev_pair) if hos == 0 else None)],
                        inc=True)
                    ph2["bank_rd"][tc2] = ph2["y2e"][(tc2, hos)]
            for tc2 in range(2):
                tc = tcb * 2 + tc2
                mv = tc2 * 2
                sbase = tc2 * 8
                t_sq4 = None
                for seg in range(4):
                    t_sq4 = S.emit("scalar",
                                   lambda s, tc2=tc2, seg=seg, sbase=sbase:
                                   s.activation(PS[4][:, :],
                                                y2t[:, tc2, seg * 512:(seg + 1) * 512],
                                                AF.Square,
                                                accum_out=bnsc[:, sbase + seg:sbase + seg + 1]),
                                   deps=([ph2["y2e"][(tc2, s_)] for s_ in range(4)]
                                         + [ph2.get(("sq_rd", tc2))]) if seg == 0 else [],
                                   inc=(seg == 3))
                t_rsum = S.emit("vector",
                                lambda v, tc2=tc2, mv=mv:
                                v.reduce_sum(bnmv[:, mv:mv + 1], y2t[:, tc2, :],
                                             axis=mybir.AxisListType.X),
                                deps=[ph2.get(("lv", tc2))], inc=True)
                t_ng = S.emit("vector",
                              lambda v, sbase=sbase, mv=mv:
                              v.tensor_scalar(small2[:, sbase:sbase + 1],
                                              bnmv[:, mv:mv + 1], cp(6), None, ALU.mult),
                              deps=[], inc=True)
                t_ssq = S.emit("vector",
                               lambda v, sbase=sbase, mv=mv:
                               v.reduce_sum(bnmv[:, mv + 1:mv + 2],
                                            bnsc[:, sbase:sbase + 4],
                                            axis=mybir.AxisListType.X),
                               deps=[t_sq4], inc=True)
                ph2[("sq_rd", tc2)] = t_ssq
                t_m2 = S.emit("vector",
                              lambda v, sbase=sbase:
                              v.tensor_tensor(small2[:, sbase + 3:sbase + 4],
                                              small2[:, sbase:sbase + 1],
                                              small2[:, sbase:sbase + 1], ALU.mult),
                              deps=[], inc=True)
                t_var = S.emit("vector",
                               lambda v, sbase=sbase, mv=mv:
                               v.scalar_tensor_tensor(small2[:, sbase + 2:sbase + 3],
                                                      bnmv[:, mv + 1:mv + 2], cp(2),
                                                      small2[:, sbase + 3:sbase + 4],
                                                      ALU.mult, ALU.subtract),
                               deps=[], inc=True)
                t_lv = S.emit("scalar",
                              lambda s, sbase=sbase:
                              s.activation(small2[:, sbase + 4:sbase + 5],
                                           small2[:, sbase + 2:sbase + 3],
                                           AF.Ln, bias=cp(3)),
                              deps=[t_var], inc=True)
                ph2[("lv", tc2)] = t_lv
                t_rs = S.emit("scalar",
                              lambda s, sbase=sbase:
                              s.activation(small2[:, sbase + 1:sbase + 2],
                                           small2[:, sbase + 4:sbase + 5],
                                           AF.Exp, scale=cp(4)),
                              deps=[t_lv, ph2.get(("x2", tc2))], inc=True)
                t_dbg = None
                if debug:
                    t_dbg = S.emit("gpsimd",
                                   lambda g, tc2=tc2, tt=tt, tc=tc:
                                   g.dma_start(dbg_y2[tt * 4 + tc, :, :], y2t[:, tc2, :]),
                                   deps=[ph2["y2e"][(tc2, hh)] for hh in range(4)],
                                   inc="gq")
                t_x2 = S.emit("vector",
                              lambda v, tc2=tc2, sbase=sbase:
                              v.tensor_scalar(y2t[:, tc2, :], y2t[:, tc2, :],
                                              small2[:, sbase:sbase + 1],
                                              small2[:, sbase + 1:sbase + 2],
                                              ALU.add, ALU.mult),
                              deps=[t_ng, t_rs, t_dbg], inc=True)
                ph2[("x2", tc2)] = t_x2
                r0 = tt * 512 + tc * 128

                def mk_out(r0, tc2, t_x2, tcb):
                    def emit_out():
                        t_out = S.emit("gpsimd",
                                       lambda g, r0=r0, tc2=tc2:
                                       g.dma_start(out_d[r0:r0 + 128, :], y2t[:, tc2, :]),
                                       deps=[t_x2], inc="out%d" % tc2)
                        ph2["outt"][(tcb, tc2)] = t_out
                        t_out_toks.append(t_out)
                    return emit_out
                ph2.setdefault("pend_out", []).append(mk_out(r0, tc2, t_x2, tcb))
        ph2["dn_done_prev"] = ("tensor", S.cnt["tensor"])
        if tt == 3:
            for f in ph2.pop("pend_out", []):
                f()

    if debug:
        S.emit("gpsimd", lambda g: g.dma_start(dbg_fuse[...], fuse[...]),
               deps=[ph2["fuse"][63]], inc="gq")

    import contextlib
    with contextlib.ExitStack() as stack:
        sems = {}
        for nm in ("tensor", "scalar", "vector") + DMA_SEMS:
            sems[nm] = stack.enter_context(nc.semaphore("sem_" + nm))
        sems["gpsimd"] = sems["gq"]
        sems["sync"] = sems["sq"]
        n_out0 = S.cnt.get("out0", 0)
        n_out1 = S.cnt.get("out1", 0)
        with nc.Block() as block:

            @block.tensor
            def _(t):
                S.replay("tensor", t, sems, {})

            @block.scalar
            def _(s):
                S.replay("scalar", s, sems, {})

            @block.vector
            def _(v):
                S.replay("vector", v, sems, {})

            @block.gpsimd
            def _(g):
                S.replay("gpsimd", g, sems, {})

            @block.sync
            def _(s):
                S.replay("sync", s, sems, {})
                s.wait_ge(sems["out0"], n_out0)
                s.wait_ge(sems["out1"], n_out1)

    return nc


_CACHE = {}


def kernel(L0_attn_X, L0_attn_Wq, L0_attn_Wk, L0_attn_Wv, L0_attn_Wo,
           L0_ffn_w_gate, L0_ffn_w_up, L0_ffn_w_down,
           L0_ffn_s_gate, L0_ffn_s_up, L0_ffn_s_down):
    X = np.asarray(L0_attn_X, dtype=np.float32)
    B = X.shape[0]
    if "nc" not in _CACHE:
        _CACHE["nc"] = build_nc()
    nc = _CACHE["nc"]
    shared = {
        "wqt": np.ascontiguousarray(np.asarray(L0_attn_Wq, dtype=np.float32).T),
        "wkt": np.ascontiguousarray(np.asarray(L0_attn_Wk, dtype=np.float32).T),
        "wvt": np.ascontiguousarray(np.asarray(L0_attn_Wv, dtype=np.float32).T),
        "wot": np.ascontiguousarray(np.asarray(L0_attn_Wo, dtype=np.float32).T),
        "wg": np.ascontiguousarray(np.asarray(L0_ffn_w_gate, dtype=np.int8)),
        "wu": np.ascontiguousarray(np.asarray(L0_ffn_w_up, dtype=np.int8)),
        "wdn": np.ascontiguousarray(np.asarray(L0_ffn_w_down, dtype=np.int8)),
        "sgp": np.ascontiguousarray(
            np.asarray(L0_ffn_s_gate, dtype=np.float32).reshape(64, 128).T),
        "sup": np.ascontiguousarray(
            np.asarray(L0_ffn_s_up, dtype=np.float32).reshape(64, 128).T),
        "sdp": np.ascontiguousarray(
            np.asarray(L0_ffn_s_down, dtype=np.float32).reshape(16, 128).T),
        "sdr": np.ascontiguousarray(
            np.asarray(L0_ffn_s_down, dtype=np.float32).reshape(1, 2048)),
    }
    in_maps = []
    for c in range(B):
        m = dict(shared)
        m["xfm"] = np.ascontiguousarray(X[c].T)
        in_maps.append(m)
    res = run_bass_kernel_spmd(nc, in_maps, core_ids=list(range(B)))
    return np.stack([res.results[c]["out"] for c in range(B)], axis=0)


# version 27
# speedup vs baseline: 1.1935x; 1.0056x over previous
import numpy as np
import ml_dtypes
import concourse.bass as bass
import concourse.mybir as mybir
from concourse.bass_utils import run_bass_kernel_spmd

F32 = mybir.dt.float32
BF16 = mybir.dt.bfloat16
I8 = mybir.dt.int8
AF = mybir.ActivationFunctionType
ALU = mybir.AluOpType
F32R = mybir.dt.float32r

T = 2048
D = 2048
NH = 16
DH = 128
IO = 8192
NJC = 16
QT = 512
NQT = T // QT
NIC = IO // 128
MSHIFT = -32.0
SCL = float(DH) ** -0.5
NEG = -1.0e30

DMA_SEMS = ("gq", "sq", "sxx", "wq", "wk", "wv", "wo0", "wo1", "xt0", "xt1",
            "gw0", "gw1", "uw0", "uw1", "dq0", "dq1", "dq2", "dq3", "out0", "out1")
ENGINES = ("tensor", "scalar", "vector", "gpsimd", "sync")


class Sched:
    def __init__(self, strict=False):
        self.ops = {e: [] for e in ENGINES}
        self.cnt = {}
        self.strict = strict

    def emit(self, eng, fn, deps=(), inc=None):
        deps2 = []
        for d in deps:
            if d is None:
                continue
            s, v = d
            if not self.strict and s == eng:
                continue
            if (s, eng) in (("gq", "gpsimd"), ("sq", "sync")):
                continue
            deps2.append((s, v))
        if self.strict and eng not in DMA_SEMS:
            if self.cnt.get(eng, 0) > 0:
                deps2.append((eng, self.cnt[eng]))
            if not inc:
                inc = True
        tok = None
        if inc:
            sem = eng if inc is True else inc
            a = 16 if sem in DMA_SEMS else 1
            self.cnt[sem] = self.cnt.get(sem, 0) + a
            tok = (sem, self.cnt[sem])
        self.ops[eng].append((deps2, fn, tok))
        return tok

    def replay(self, eng_name, eng, sems, wm):
        for deps, fn, tok in self.ops[eng_name]:
            coll = {}
            for s, v in deps:
                if v > coll.get(s, 0):
                    coll[s] = v
            for s, v in coll.items():
                if wm.get(s, 0) < v:
                    eng.wait_ge(sems[s], v)
                    wm[s] = v
            ins = fn(eng)
            if tok is not None:
                s, v = tok
                ins.then_inc(sems[s], 16 if s in DMA_SEMS else 1)


def build_nc(debug=False):
    import os
    strict = os.environ.get("BASS_SIM_FP64") == "1"
    nc = bass.Bass("TRN2", target_bir_lowering=False)

    xfm = nc.dram_tensor("xfm", [D, T], F32, kind="ExternalInput").ap()
    wqt = nc.dram_tensor("wqt", [D, D], F32, kind="ExternalInput").ap()
    wkt = nc.dram_tensor("wkt", [D, D], F32, kind="ExternalInput").ap()
    wvt = nc.dram_tensor("wvt", [D, D], F32, kind="ExternalInput").ap()
    wot = nc.dram_tensor("wot", [D, D], F32, kind="ExternalInput").ap()
    wg = nc.dram_tensor("wg", [D, IO], I8, kind="ExternalInput").ap()
    wu = nc.dram_tensor("wu", [D, IO], I8, kind="ExternalInput").ap()
    wdn = nc.dram_tensor("wdn", [IO, D], I8, kind="ExternalInput").ap()
    sgp_d = nc.dram_tensor("sgp", [128, 64], F32, kind="ExternalInput").ap()
    sup_d = nc.dram_tensor("sup", [128, 64], F32, kind="ExternalInput").ap()
    sdp_d = nc.dram_tensor("sdp", [128, 16], F32, kind="ExternalInput").ap()
    sdr_d = nc.dram_tensor("sdr", [1, D], F32, kind="ExternalInput").ap()
    out_d = nc.dram_tensor("out", [T, D], F32, kind="ExternalOutput").ap()
    wdnb = nc.dram_tensor("wdnb", [IO, D], BF16, kind="Internal").ap()

    th = 10000.0 ** (-2.0 * np.arange(DH // 2, dtype=np.float64) / DH)
    ang = np.arange(T, dtype=np.float64)[:, None] * th[None, :]
    cosv = np.cos(ang).astype(np.float32).T
    sinv = np.sin(ang).astype(np.float32).T
    cosF = np.empty((128, T), dtype=np.float32)
    sinF = np.empty((128, T), dtype=np.float32)
    cosF[0::2, :] = cosv
    cosF[1::2, :] = cosv
    sinF[0::2, :] = -sinv
    sinF[1::2, :] = sinv
    pswap = np.zeros((128, 128), dtype=np.float32)
    for i in range(64):
        pswap[2 * i + 1, 2 * i] = 1.0
        pswap[2 * i, 2 * i + 1] = 1.0
    tri = np.zeros((128, 128), dtype=np.float32)
    for p in range(128):
        tri[p, :p] = NEG
    ident = np.eye(128, dtype=np.float32)
    onescol = np.ones((128, 1), dtype=np.float32)
    onesrow = np.ones((1, 128), dtype=np.float32)
    cpool_v = np.array([MSHIFT, SCL, 1.0 / D, 1e-5, -0.5, -1.0, -1.0 / D, 0.0],
                       dtype=np.float32)[None, :].repeat(128, axis=0)

    cosF_d = nc.inline_tensor(np.ascontiguousarray(cosF), "cosFc").ap()
    sinF_d = nc.inline_tensor(np.ascontiguousarray(sinF), "sinFc").ap()
    pswap_d = nc.inline_tensor(pswap, "pswapc").ap()
    tri_d = nc.inline_tensor(tri, "tric").ap()
    ident_d = nc.inline_tensor(ident, "identc").ap()
    onescol_d = nc.inline_tensor(onescol, "onescolc").ap()
    onesrow_d = nc.inline_tensor(onesrow, "onesrowc").ap()
    cpool_d = nc.inline_tensor(np.ascontiguousarray(cpool_v), "cpoolc").ap()

    BASE = 16512
    R_A = BASE
    R_B = BASE + 65536
    R_W = BASE + 131072
    R_K = BASE + 196608
    at = nc.alloc_sbuf_tensor_at

    xbf = at("xbf", [128, NJC, T], BF16, offset=R_A)
    heads = at("heads", [128, NJC, T], BF16, offset=R_B)
    fuse = at("fuse", [128, NIC, QT], BF16, offset=R_B)

    cosf = at("cosf", [128, T], BF16, offset=R_W + 0)
    sinf = at("sinf", [128, T], BF16, offset=R_W + 4096)
    qraw = at("qraw", [128, T], BF16, offset=R_W + 8192)
    qr = at("qr", [128, 2, T], BF16, offset=R_W + 12288)
    kr = at("kr", [128, 2, T], BF16, offset=R_W + 20480)
    pt = at("pt", [128, 4, QT], BF16, offset=R_W + 28672)
    vtm = at("vtm", [128, 16, 256], BF16, offset=R_W + 32768)
    wqs = at("wqs", [128, NJC, 128], BF16, offset=R_W + 40960)
    wks = at("wks", [128, NJC, 128], BF16, offset=R_W + 45056)
    wvs = at("wvs", [128, NJC, 256], BF16, offset=R_W + 49152)
    rsc1 = at("rsc1", [128, 512], F32, offset=R_W + 57344)
    rsc2 = at("rsc2", [128, 512], F32, offset=R_W + 59392)
    sdrow = at("sdrow", [1, D], F32, offset=R_W + 57344)

    wo_init = at("wo_init", [128, 2, NJC, 128], BF16, offset=R_W + 0)
    ybf = at("ybf", [128, NJC, 512], BF16, offset=R_W + 0)
    ysq = at("ysq", [128, 512], BF16, offset=R_W + 16384)
    wos = at("wos", [128, 2, NJC, 128], BF16, offset=R_W + 17408)
    xtile = at("xtile", [128, 2, 512], F32, offset=R_W + 25600)
    m_sb = at("m_sb", [128, 512], BF16, offset=R_W + 29696)
    r_sb = at("r_sb", [128, 512], BF16, offset=R_W + 30720)

    dwn = at("dwn", [128, 4, 4, 512], BF16, offset=R_W + 0)
    y2t = at("y2t", [128, 2, D], BF16, offset=R_W + 16384)
    gws = at("gws", [128, 2, NJC, 256], BF16, offset=R_W + 24576)
    uws = at("uws", [128, 2, NJC, 256], BF16, offset=R_W + 40960)
    diag = at("diag", [128, 16, 128], BF16, offset=R_W + 57344)
    gsc = at("gsc", [128, 512], F32, offset=R_W + 61440)
    usc = at("usc", [128, 512], F32, offset=R_W + 63488)

    s_bc = at("s_bc", [128, D], F32, offset=R_K + 0)
    pswap_sb = at("pswap_sb", [128, 128], BF16, offset=R_K + 8192)
    tri_sb = at("tri_sb", [128, 128], BF16, offset=R_K + 8448)
    ident_sb = at("ident_sb", [128, 128], BF16, offset=R_K + 8704)
    onescol_sb = at("onescol_sb", [128, 1], BF16, offset=R_K + 8960)
    onesrow_sb = at("onesrow_sb", [1, 128], F32, offset=R_K + 8992)
    cpool_sb = at("cpool_sb", [128, 8], F32, offset=R_K + 9504)
    sg_sb = at("sg_sb", [128, 64], F32, offset=R_K + 9536)
    su_sb = at("su_sb", [128, 64], F32, offset=R_K + 9792)
    rowbuf = at("rowbuf", [1, QT], F32, offset=R_K + 10048)
    rowbuf2 = at("rowbuf2", [1, QT], F32, offset=R_K + 12096)
    recipbc = at("recipbc", [128, QT], F32, offset=R_K + 14144)
    bnsc = at("bnsc", [128, 24], F32, offset=R_K + 14144)
    bnmv = at("bnmv", [128, 4], F32, offset=R_K + 14240)
    small2 = at("small2", [128, 16], F32, offset=R_K + 14272)
    sdpp = at("sdpp", [128, 16], F32, offset=R_K + 14336)
    rsdp = at("rsdp", [128, 16], F32, offset=R_K + 14400)
    assert R_K + 16192 <= 229344

    PSD01 = nc.alloc_psum_tensor("psd01", [128, 1024], F32)
    PSD23 = nc.alloc_psum_tensor("psd23", [128, 1024], F32)
    PS = [PSD01[:, 0:512], PSD01[:, 512:1024],
          PSD23[:, 0:512], PSD23[:, 512:1024]] + [
        nc.alloc_psum_tensor("psb%d" % i, [128, 512], F32)[...] for i in range(4, 8)]
    PSDP = [PSD01, PSD23]

    if debug:
        dbg_x1 = nc.dram_tensor("dbg_x1", [128, NJC, T], F32, kind="ExternalOutput").ap()
        dbg_heads = nc.dram_tensor("dbg_heads", [128, NJC, T], F32, kind="ExternalOutput").ap()
        dbg_fuse = nc.dram_tensor("dbg_fuse", [128, NIC, QT], F32, kind="ExternalOutput").ap()
        dbg_y2 = nc.dram_tensor("dbg_y2", [16, 128, D], F32, kind="ExternalOutput").ap()

    S = Sched(strict=strict)

    def cp(idx, p=128):
        return cpool_sb[0:p, idx:idx + 1]

    xfm_r = xfm.rearrange("(jc p) t -> p jc t", p=128)
    wqt_r = wqt.rearrange("(jc p) i -> p jc i", p=128)
    wkt_r = wkt.rearrange("(jc p) i -> p jc i", p=128)
    wvt_r = wvt.rearrange("(jc p) i -> p jc i", p=128)
    wot_r = wot.rearrange("(jc p) i -> p jc i", p=128)
    wg_r = wg.rearrange("(hc p) io -> p hc io", p=128)
    wu_r = wu.rearrange("(hc p) io -> p hc io", p=128)
    wdn_r = wdn.rearrange("(icb ic p) ho -> p icb ic ho", p=128, ic=4)
    wdnb_r = wdnb.rearrange("(icb ic p) ho -> p icb ic ho", p=128, ic=4)

    t_cn = {}
    for nm, dst, src in (("cos", cosf, cosF_d), ("sin", sinf, sinF_d),
                         ("psw", pswap_sb, pswap_d), ("tri", tri_sb, tri_d),
                         ("idn", ident_sb, ident_d), ("onc", onescol_sb, onescol_d)):
        t_cn[nm] = S.emit("gpsimd", lambda g, a=dst, b=src: g.dma_start(a[...], b), inc="gq")
    for nm in ("cos", "sin", "psw", "tri", "idn", "onc"):
        t_cn[nm] = t_cn["onc"]
    for nm, dst, src in (("onr", onesrow_sb, onesrow_d), ("cpl", cpool_sb, cpool_d),
                         ("sg", sg_sb, sgp_d), ("su", su_sb, sup_d), ("sdr", sdrow, sdr_d)):
        t_cn[nm] = S.emit("sync", lambda s, a=dst, b=src: s.dma_start(a[...], b), inc="sq")
    for nm in ("onr", "cpl", "sg", "su", "sdr"):
        t_cn[nm] = t_cn["sdr"]

    t_x = []
    for j4 in range(4):
        t_x.append(S.emit("gpsimd",
                          lambda g, j4=j4: g.dma_start(xbf[:, j4 * 4:(j4 + 1) * 4, :],
                                                       xfm_r[:, j4 * 4:(j4 + 1) * 4, :]),
                          inc="sxx"))
    t_xc = t_x[-1]

    t_sbc = None
    for n4 in range(4):
        tm = S.emit("tensor",
                    lambda t, n4=n4: t.matmul(PS[6], onesrow_sb[:, :],
                                              sdrow[0:1, n4 * 512:(n4 + 1) * 512],
                                              start=True, stop=True, skip_group_check=True),
                    deps=[t_cn["onr"], t_cn["sdr"], t_sbc], inc=True)
        t_sbc = S.emit("scalar",
                       lambda s, n4=n4: s.copy(s_bc[:, n4 * 512:(n4 + 1) * 512], PS[6][:, :]),
                       deps=[tm], inc=True)

    def load_wslice(dst_ap, src_r, c0, c1, deps, sem="gq"):
        tk = None
        for h2 in range(2):
            tk = S.emit("gpsimd",
                        lambda g, h2=h2, dst_ap=dst_ap, src_r=src_r, c0=c0, c1=c1:
                        g.dma_start(dst_ap[:, h2 * 8:(h2 + 1) * 8, :],
                                    src_r[:, h2 * 8:(h2 + 1) * 8, c0:c1]),
                        deps=deps if h2 == 0 else [], inc=sem)
        return tk

    st = {
        "projbank": None,
        "ps5_rd": None,
        "rope": {},
        "lastS": {},
        "wq": {}, "wk": {}, "wv": {},
        "qk_mm_done": {},
        "v_mm_done": {},
        "vtm_ev": {},
        "exp_bank": {},
        "pt_rd": {},
        "sums_free": None,
        "rb2_rd": None,
        "ps6_rd": None,
        "recip_rd": None,
        "pv_free": None,
        "qraw_rd": None,
        "hevac": {},
        "rsc_rd": None,
    }

    st["wq"][0] = load_wslice(wqs, wqt_r, 0, 128, [t_cn["sdr"]], sem="wq")
    st["wk"][0] = load_wslice(wks, wkt_r, 0, 128, [], sem="wk")
    st["wv"][0] = load_wslice(wvs, wvt_r, 0, 256, [t_sbc], sem="wv")

    def emit_qk_proj(h, which):
        par = h % 2
        dst = qr if which == "q" else kr
        w_sb = wqs if which == "q" else wks
        t_w = st["wq" if which == "q" else "wk"][h]
        last_mm = None
        for tt in range(4):
            sl = slice(tt * 512, (tt + 1) * 512)
            tmm = None
            for jc in range(NJC):
                d = []
                if jc == 0:
                    d = [t_w, st["projbank"], t_xc]
                tmm = S.emit("tensor",
                             lambda t, jc=jc, sl=sl, w_sb=w_sb:
                             t.matmul(PS[7][:, :], w_sb[:, jc, :], xbf[:, jc, sl],
                                      start=(jc == 0), stop=(jc == NJC - 1),
                                      skip_group_check=True),
                             deps=d, inc=(jc == NJC - 1))
            last_mm = tmm
            tev = S.emit("vector",
                         lambda v, sl=sl: v.tensor_copy(qraw[:, sl], PS[7]),
                         deps=[tmm, st["qraw_rd"]], inc=True)
            st["projbank"] = tev
            tqs = S.emit("tensor",
                         lambda t, sl=sl: t.matmul(PS[3][:, :], pswap_sb[:, :], qraw[:, sl],
                                                   start=True, stop=True,
                                                   skip_group_check=True),
                         deps=[tev, st["ps5_rd"], st["exp_bank"].get(1),
                               t_cn["psw"]], inc=True)
            ta = S.emit("vector",
                        lambda v, sl=sl: v.tensor_tensor(rsc1[:, :], qraw[:, sl], cosf[:, sl],
                                                         ALU.mult),
                        deps=[tev, t_cn["cos"], st["rsc_rd"], t_sbc], inc=True)
            tb = S.emit("vector",
                        lambda v, sl=sl: v.tensor_tensor(rsc2[:, :], PS[3][:, :], sinf[:, sl],
                                                         ALU.mult),
                        deps=[tqs, t_cn["sin"]], inc=True)
            st["ps5_rd"] = tb
            tc_ = S.emit("vector",
                         lambda v, sl=sl, dst=dst, par=par:
                         v.tensor_tensor(dst[:, par, sl], rsc1[:, :], rsc2[:, :], ALU.add),
                         deps=[st["lastS"].get(h - 2)], inc=True)
            st["rsc_rd"] = tc_
            st["qraw_rd"] = tc_
            st["rope"][(h, which, tt)] = tc_
        st["qk_mm_done"][h] = ("tensor", S.cnt["tensor"])
        return last_mm

    def emit_prep(h):
        emit_qk_proj(h, "q")
        emit_qk_proj(h, "k")
        if h + 1 < NH:
            st["wq"][h + 1] = load_wslice(wqs, wqt_r, (h + 1) * 128, (h + 2) * 128,
                                          [st["qk_mm_done"][h]], sem="wq")
            st["wk"][h + 1] = load_wslice(wks, wkt_r, (h + 1) * 128, (h + 2) * 128,
                                          [st["qk_mm_done"][h]], sem="wk")


    def emit_vproj(g):
        for tcc in range(16):
            tmm = None
            for jc in range(NJC):
                d = []
                if jc == 0:
                    d = [st["wv"][g], t_xc, st["projbank"]]
                tmm = S.emit("tensor",
                             lambda t, jc=jc, tcc=tcc:
                             t.matmul(PS[7][:, 0:256], xbf[:, jc, tcc * 128:(tcc + 1) * 128],
                                      wvs[:, jc, :],
                                      start=(jc == 0), stop=(jc == NJC - 1),
                                      skip_group_check=True),
                             deps=d, inc=(jc == NJC - 1))
            tev = S.emit("vector",
                         lambda v, tcc=tcc: v.tensor_copy(vtm[:, tcc, :], PS[7][:, 0:256]),
                         deps=[tmm], inc=True)
            st["projbank"] = tev
        st["v_mm_done"][g] = ("tensor", S.cnt["tensor"])
        st["vtm_ev"][g] = ("vector", S.cnt["vector"])
        if g + 1 < 8:
            st["wv"][g + 1] = load_wslice(wvs, wvt_r, (g + 1) * 256, (g + 2) * 256,
                                          [st["v_mm_done"][g]], sem="wv")

    def emit_attn(h):
        par = h % 2
        g = h // 2
        off = (h % 2) * 128
        for qt in range(NQT):
            nchunks = 4 * qt + 4
            qsl = slice(qt * 512, (qt + 1) * 512)
            sums_ps = PS[5]
            t_S = {}
            t_tri = {}
            t_sum_last = None
            t_pv_last = None
            pending = {}

            def emit_S_pair(cp_i):
                cc = [2 * cp_i, 2 * cp_i + 1]
                slots = [c % 4 for c in cc]
                pp = cp_i % 2
                for c in cc:
                    bank = 2 * pp + (c % 2)
                    d = [st["rope"][(h, "k", c // 4)], st["exp_bank"].get(pp)]
                    if pp == 1 and c % 2 == 0:
                        d.append(st["ps5_rd"])
                    if c == 0:
                        d += [st["rope"][(h, "q", qt)], st["vtm_ev"][g]]
                    t_S[c] = S.emit("tensor",
                                    lambda t, c=c, bank=bank, par=par, qsl=qsl:
                                    t.matmul(PS[bank],
                                             kr[:, par, c * 128:(c + 1) * 128],
                                             qr[:, par, qsl],
                                             start=True, stop=True, skip_group_check=True),
                                    deps=d, inc=True)
                for c in cc:
                    if c >= 4 * qt:
                        bank = 2 * pp + (c % 2)
                        c0 = (c - 4 * qt) * 128
                        t_tri[c] = S.emit("vector",
                                          lambda v, c=c, bank=bank, c0=c0:
                                          v.tensor_tensor(PS[bank][:, c0:c0 + 128],
                                                          PS[bank][:, c0:c0 + 128],
                                                          tri_sb[:, :], ALU.add),
                                          deps=[t_S[c], t_cn["tri"]], inc=True)
                t_exp = {}
                t_ms = {}
                for c in cc:
                    bank = 2 * pp + (c % 2)
                    slot = c % 4
                    c0 = (c - 4 * qt) * 128 if c >= 4 * qt else 0
                    if c0 > 0:
                        t_ms[c] = S.emit("vector",
                                         lambda v, slot=slot, c0=c0:
                                         v.memset(pt[:, slot, 0:c0], 0.0),
                                         deps=[st["pt_rd"].get(slot)], inc=True)
                    d = [t_S[c], t_tri.get(c), st["pt_rd"].get(slot), t_cn["cpl"]]
                    t_exp[c] = S.emit("scalar",
                                      lambda s, slot=slot, bank=bank, c0=c0:
                                      s.activation(pt[:, slot, c0:], PS[bank][:, c0:],
                                                   AF.Exp, bias=cp(0), scale=cp(1)),
                                      deps=d, inc=True)
                st["exp_bank"][pp] = t_exp[cc[1]]
                pending[cp_i] = (cc, t_exp, t_ms)

            def emit_consume_pair(cp_i):
                nonlocal t_sum_last, t_pv_last
                cc, t_exp, t_ms = pending.pop(cp_i)
                for c in cc:
                    slot = c % 4
                    d = [t_exp[c], t_ms.get(c)]
                    if c == 0:
                        d.append(st["sums_free"])
                    t_sum_last = S.emit("tensor",
                                        lambda t, c=c, slot=slot, sums_ps=sums_ps,
                                        nchunks=nchunks:
                                        t.matmul(sums_ps[0:1, :], onescol_sb[:, :],
                                                 pt[:, slot, :],
                                                 start=(c == 0), stop=(c == nchunks - 1),
                                                 skip_group_check=True),
                                        deps=d + [t_cn["onc"]], inc=(c == nchunks - 1))
                    d = []
                    if c == 0:
                        d = [st["pv_free"]]
                    t_pv_last = S.emit("tensor",
                                       lambda t, c=c, slot=slot, off=off, nchunks=nchunks:
                                       t.matmul(PS[4], vtm[:, c, off:off + 128],
                                                pt[:, slot, :],
                                                start=(c == 0), stop=(c == nchunks - 1),
                                                skip_group_check=True),
                                       deps=d, inc=True)
                    st["pt_rd"][slot] = t_pv_last

            for cp_i in range(nchunks // 2):
                emit_S_pair(cp_i)
                if cp_i == 0 and st.get("pend_chain"):
                    st.pop("pend_chain")()
                if cp_i > 0:
                    emit_consume_pair(cp_i - 1)
            emit_consume_pair(nchunks // 2 - 1)
            st["lastS"][h] = ("tensor", S.cnt["tensor"])
            t_ln = S.emit("scalar",
                          lambda s, sums_ps=sums_ps: s.activation(rowbuf[0:1, :],
                                                                  sums_ps[0:1, :], AF.Ln),
                          deps=[t_sum_last], inc=True)
            st["sums_free"] = t_ln
            t_rc = S.emit("scalar",
                          lambda s: s.activation(rowbuf2[0:1, :], rowbuf[0:1, :],
                                                 AF.Exp, scale=cp(5, 1)),
                          deps=[st["rb2_rd"]], inc=True)

            def mk_chain(h, qt, qsl, t_rc, t_pv_last):
                def emit_chain():
                    t_bc = S.emit("tensor",
                                  lambda t: t.matmul(PS[6], onesrow_sb[:, :],
                                                     rowbuf2[0:1, :],
                                                     start=True, stop=True,
                                                     skip_group_check=True),
                                  deps=[t_rc, st["ps6_rd"]], inc=True)
                    st["rb2_rd"] = t_bc
                    t_cpy = S.emit("vector",
                                   lambda v: v.tensor_copy(recipbc[:, :], PS[6]),
                                   deps=[t_bc, st["recip_rd"]], inc=True)
                    st["ps6_rd"] = t_cpy
                    t_hv = S.emit("vector",
                                  lambda v, h=h, qsl=qsl:
                                  v.tensor_tensor(heads[:, h, qsl], PS[4], recipbc[:, :],
                                                  ALU.mult),
                                  deps=[t_pv_last, t_cpy], inc=True)
                    st["recip_rd"] = t_hv
                    st["pv_free"] = t_hv
                    st["hevac"][(h, qt)] = t_hv
                return emit_chain
            st["pend_chain"] = mk_chain(h, qt, qsl, t_rc, t_pv_last)

    st["ps6_rd"] = t_sbc
    emit_prep(0)
    emit_vproj(0)
    for h in range(NH):
        if h + 1 < NH:
            emit_prep(h + 1)
        emit_attn(h)
        if h % 2 == 1 and h // 2 + 1 < 8:
            emit_vproj(h // 2 + 1)

    if st.get("pend_chain"):
        st.pop("pend_chain")()
    t_attn = st["hevac"][(NH - 1, NQT - 1)]

    t_wo_ld = {}
    t_wo_ld[0] = load_wslice(wo_init[:, 0], wot_r, 0, 128,
                             [st["rope"][(NH - 1, "k", 3)]], sem="wo0")
    t_wo_ld[1] = load_wslice(wo_init[:, 1], wot_r, 128, 256, [], sem="wo1")
    t_xt = {}
    t_womm = {}
    t_yadd = {}
    t_stats = {}
    ln1 = {"ps6_rd": st["ps6_rd"], "ps7_rd": st["projbank"], "x1": {},
           "m_rd": None, "r_rd": st["rb2_rd"], "mcp": None, "rcp": None,
           "pending": None}

    for tt in range(4):
        tsl = slice(tt * 512, (tt + 1) * 512)
        t_cs = None
        t_sq = None
        for ic in range(NJC):
            n = tt * 16 + ic
            par = ic % 2
            bank = 4 + par
            if ln1["pending"] is not None:
                ln1["pending"](ic)
            t_xt[(tt, ic)] = S.emit("sync",
                                    lambda s, ic=ic, par=par, tsl=tsl:
                                    s.dma_start(xtile[:, par, :], xfm_r[:, ic, tsl]),
                                    deps=[t_attn if n < 2 else None,
                                          t_yadd.get((tt, ic - 2)) or
                                          (t_yadd.get((tt - 1, 14 + ic)) if tt > 0 else None)],
                                    inc="xt%d" % par)
            tmm = None
            for jc in range(NJC):
                d = []
                if jc == 0:
                    d = [t_wo_ld[n], t_attn, st["sums_free"],
                         t_yadd.get((tt, ic - 2)) or
                         (t_yadd.get((tt - 1, 14 + ic)) if tt > 0 else None)]
                wsrc = wo_init if n < 2 else wos
                tmm = S.emit("tensor",
                             lambda t, jc=jc, ic=ic, bank=bank, tsl=tsl, wsrc=wsrc:
                             t.matmul(PS[bank][:, :], wsrc[:, ic % 2, jc, :],
                                      heads[:, jc, tsl],
                                      start=(jc == 0), stop=(jc == NJC - 1),
                                      skip_group_check=True),
                             deps=d, inc=(jc == NJC - 1))
            t_womm[(tt, ic)] = tmm
            if n + 2 < 64:
                nic = (n + 2) % 16
                t_wo_ld[n + 2] = load_wslice(wos[:, (n + 2) % 2], wot_r,
                                             nic * 128, (nic + 1) * 128, [tmm],
                                             sem="wo%d" % ((n + 2) % 2))
            t_yadd[(tt, ic)] = S.emit("vector",
                                      lambda v, ic=ic, par=par, bank=bank:
                                      v.tensor_tensor(ybf[:, ic, :], PS[bank][:, :],
                                                      xtile[:, par, :], ALU.add),
                                      deps=[tmm, t_xt[(tt, ic)]],
                                      inc=True)
            t_ysq = S.emit("vector",
                           lambda v, ic=ic: v.tensor_tensor(ysq[:, :], ybf[:, ic, :],
                                                            ybf[:, ic, :], ALU.mult),
                           deps=[t_stats.get("sq")], inc=True)
            t_cs = S.emit("tensor",
                          lambda t, ic=ic: t.matmul(PS[6][0:1, :], onescol_sb[:, :],
                                                    ybf[:, ic, :],
                                                    start=(ic == 0), stop=(ic == NJC - 1),
                                                    skip_group_check=True),
                          deps=[t_yadd[(tt, ic)]] + ([ln1["ps6_rd"]] if ic == 0 else []),
                          inc=(ic == NJC - 1))
            t_sq = S.emit("tensor",
                          lambda t, ic=ic: t.matmul(PS[7][0:1, :], onescol_sb[:, :],
                                                    ysq[:, :],
                                                    start=(ic == 0), stop=(ic == NJC - 1),
                                                    skip_group_check=True),
                          deps=[t_ysq] + ([ln1["ps7_rd"]] if ic == 0 else []),
                          inc=(ic == NJC - 1))
            t_stats["sq"] = t_sq
        t_m = S.emit("scalar",
                     lambda s: s.activation(rowbuf[0:1, :], PS[6][0:1, :], AF.Copy,
                                            scale=cp(2, 1)),
                     deps=[t_cs, ln1["m_rd"]], inc=True)
        t_msq = S.emit("scalar",
                       lambda s: s.activation(rowbuf2[0:1, :], PS[7][0:1, :], AF.Copy,
                                              scale=cp(2, 1)),
                       deps=[t_sq, ln1["r_rd"]], inc=True)
        t_m2 = S.emit("vector",
                      lambda v: v.tensor_tensor(recipbc[0:1, :], rowbuf[0:1, :],
                                                rowbuf[0:1, :], ALU.mult),
                      deps=[t_m, st["recip_rd"]], inc=True)
        t_var = S.emit("vector",
                       lambda v: v.tensor_tensor(rowbuf2[0:1, :], rowbuf2[0:1, :],
                                                 recipbc[0:1, :], ALU.subtract),
                       deps=[t_msq], inc=True)
        t_lnv = S.emit("scalar",
                       lambda s: s.activation(recipbc[0:1, :], rowbuf2[0:1, :], AF.Ln,
                                              bias=cp(3, 1)),
                       deps=[t_var, t_m2], inc=True)
        t_rstd = S.emit("scalar",
                        lambda s: s.activation(rowbuf2[0:1, :], recipbc[0:1, :], AF.Exp,
                                               scale=cp(4, 1)),
                        deps=[t_lnv], inc=True)
        t_mbc = S.emit("tensor",
                       lambda t: t.matmul(PS[2], onesrow_sb[:, :], rowbuf[0:1, :],
                                          start=True, stop=True, skip_group_check=True),
                       deps=[t_m, t_m2,
                             (t_attn if tt == 0 else ln1["mcp"])],
                       inc=True)
        ln1["m_rd"] = t_mbc
        t_rbc = S.emit("tensor",
                       lambda t: t.matmul(PS[3], onesrow_sb[:, :], rowbuf2[0:1, :],
                                          start=True, stop=True, skip_group_check=True),
                       deps=[t_rstd,
                             (t_attn if tt == 0 else ln1["rcp"])],
                       inc=True)
        ln1["r_rd"] = t_rbc
        st["recip_rd"] = t_lnv
        t_mcp = S.emit("vector", lambda v: v.tensor_copy(m_sb[:, :], PS[2]),
                       deps=[t_mbc], inc=True)
        ln1["mcp"] = t_mcp
        t_rcp = S.emit("vector", lambda v: v.tensor_copy(r_sb[:, :], PS[3]),
                       deps=[t_rbc], inc=True)
        ln1["rcp"] = t_rcp

        def mk_x1(tt, tsl):
            def emit_x1(ic):
                S.emit("vector",
                       lambda v, ic=ic: v.tensor_tensor(xbf[:, ic, tsl], ybf[:, ic, :],
                                                        m_sb[:, :], ALU.subtract),
                       deps=[], inc=True)
                ln1["x1"][(tt, ic)] = S.emit(
                    "vector",
                    lambda v, ic=ic: v.tensor_tensor(xbf[:, ic, tsl], xbf[:, ic, tsl],
                                                     r_sb[:, :], ALU.mult),
                    deps=[], inc=True)
            return emit_x1
        ln1["pending"] = mk_x1(tt, tsl)
        if tt == 3:
            for ic in range(NJC):
                ln1["pending"](ic)
        ln1["ps6_rd"] = t_m
        ln1["ps7_rd"] = t_msq
    t_ln1_done = ln1["x1"][(3, 15)]
    t_wo_mm_all = ("tensor", S.cnt["tensor"])

    if debug:
        S.emit("gpsimd", lambda g: g.dma_start(dbg_heads[...], heads[...]),
               deps=[t_attn], inc="gq")
        S.emit("gpsimd", lambda g: g.dma_start(dbg_x1[...], xbf[...]),
               deps=[t_ln1_done], inc="gq")

    t_sdp = S.emit("sync", lambda s: s.dma_start(sdpp[...], sdp_d),
                   deps=[t_ln1_done, st["recip_rd"]], inc="sq")
    t_lnr = S.emit("scalar", lambda s: s.activation(rsdp[:, :], sdpp[:, :], AF.Ln),
                   deps=[t_sdp, t_ln1_done], inc=True)
    t_rsd = S.emit("scalar", lambda s: s.activation(sdpp[:, :], rsdp[:, :], AF.Exp,
                                                    scale=cp(5)),
                   deps=[t_lnr], inc=True)
    t_diag = None
    for dc in range(16):
        t_diag = S.emit("vector",
                        lambda v, dc=dc: v.tensor_scalar_mul(diag[:, dc, :], ident_sb[:, :],
                                                             sdpp[:, dc:dc + 1]),
                        deps=[t_rsd, t_cn["idn"], t_womm[(3, 15)], t_ln1_done], inc=True)

    ph2 = {"gw": {}, "uw": {},
           "silu": {}, "usc": {}, "fuse": {}, "dw_rd": {},
           "y2e": {}, "outt": {},
           "bank_rd": {0: t_ln1_done, 1: t_ln1_done}}

    def load_gu(which, s_idx, deps):
        src = wg_r if which == "g" else wu_r
        dst = gws if which == "g" else uws
        par = s_idx % 2
        sem = ("gw%d" if which == "g" else "uw%d") % par
        tk = None
        for h2 in range(2):
            tk = S.emit("gpsimd",
                        lambda g, h2=h2, dst=dst, src=src, s_idx=s_idx, par=par:
                        g.dma_start(dst[:, par, h2 * 8:(h2 + 1) * 8, :],
                                    src[:, h2 * 8:(h2 + 1) * 8,
                                        s_idx * 256:(s_idx + 1) * 256]),
                        deps=deps if h2 == 0 else [], inc=sem)
        return tk

    ph2["gw"][0] = load_gu("g", 0, [t_ln1_done])
    ph2["uw"][0] = load_gu("u", 0, [t_ln1_done])
    ph2["gw"][1] = load_gu("g", 1, [t_ln1_done])
    ph2["uw"][1] = load_gu("u", 1, [t_ln1_done])

    t_out_toks = []
    for tt in range(4):
        tsl = slice(tt * 512, (tt + 1) * 512)
        if tt > 0:
            for nx2, (tg_, tu_) in ph2.pop("nextgw", {}).items():
                ph2["gw"][nx2] = tg_
                ph2["uw"][nx2] = tu_
        for s_idx in range(32):
            for icw in range(2):
                ic = s_idx * 2 + icw
                tg = None
                for hc in range(NJC):
                    d = []
                    if hc == 0:
                        d = [ph2["gw"].get(s_idx) if icw == 0 else None, t_ln1_done,
                             ph2["silu"].get(ic - 1, ph2.get("last_silu"))]
                    tg = S.emit("tensor",
                                lambda t, hc=hc, ic=ic, s_idx=s_idx, icw=icw, tsl=tsl:
                                t.matmul(PS[0][:, :], gws[:, s_idx % 2, hc, icw * 128:(icw + 1) * 128],
                                         xbf[:, hc, tsl],
                                         start=(hc == 0), stop=(hc == NJC - 1),
                                         skip_group_check=True),
                                deps=d, inc=(hc == NJC - 1))
                if icw == 1:
                    ph2["g_mm_last"] = tg
                ph2["silu"][ic] = S.emit("scalar",
                                         lambda s, ic=ic:
                                         s.activation(gsc[:, :], PS[0][:, :],
                                                      AF.Silu,
                                                      scale=sg_sb[:, ic:ic + 1]),
                                         deps=[tg, ph2["fuse"].get(ic - 1), t_cn["sg"],
                                               (t_wo_mm_all if ic == 0 and tt == 0 else None)],
                                         inc=True)
                tu = None
                for hc in range(NJC):
                    d = []
                    if hc == 0:
                        d = [ph2["uw"].get(s_idx) if icw == 0 else None,
                             ph2["usc"].get(ic - 1, ph2.get("last_usc"))]
                    tu = S.emit("tensor",
                                lambda t, hc=hc, icw=icw, tsl=tsl, s_idx=s_idx:
                                t.matmul(PS[1][:, :], uws[:, s_idx % 2, hc, icw * 128:(icw + 1) * 128],
                                         xbf[:, hc, tsl],
                                         start=(hc == 0), stop=(hc == NJC - 1),
                                         skip_group_check=True),
                                deps=d, inc=(hc == NJC - 1))
                if icw == 1:
                    ph2["u_mm_last"] = tu
                ph2["usc"][ic] = S.emit("vector",
                                        lambda v, ic=ic:
                                        v.tensor_scalar_mul(usc[:, :], PS[1][:, :],
                                                            su_sb[:, ic:ic + 1]),
                                        deps=[tu, ph2["fuse"].get(ic - 1), t_cn["su"],
                                              (t_diag if ic == 0 and tt == 0 else None)],
                                        inc=True)
                ph2["fuse"][ic] = S.emit("vector",
                                         lambda v, ic=ic:
                                         v.tensor_tensor(fuse[:, ic, :], gsc[:, :],
                                                         usc[:, :], ALU.mult),
                                         deps=[ph2["silu"][ic],
                                               (t_wo_mm_all if (ic == 0 and tt == 0)
                                                else None),
                                               (ph2.get("dn_done_prev") if tt > 0
                                                else None)],
                                         inc=True)
            nxt = s_idx + 2
            if nxt < 32:
                ph2["gw"][nxt] = load_gu("g", nxt, [ph2["g_mm_last"]])
                ph2["uw"][nxt] = load_gu("u", nxt, [ph2["u_mm_last"]])
            else:
                nx2 = nxt - 32
                if tt < 3:
                    ph2["nextgw"] = ph2.get("nextgw", {})
                    ph2["nextgw"][nx2] = (load_gu("g", nx2, [ph2["g_mm_last"]]),
                                          load_gu("u", nx2, [ph2["u_mm_last"]]))
        ph2["last_silu"] = ph2["silu"][63]
        ph2["last_usc"] = ph2["usc"][63]
        t_fuse_all = ph2["fuse"][63]
        for tcb in range(2):
            for hos in range(4):
                t_dn_stop = {}
                for icb in range(16):
                    ph2["dwctr"] = ph2.get("dwctr", 0) + 1
                    par = ph2["dwctr"] % 4
                    t_dw = S.emit("gpsimd",
                                  lambda g, par=par, icb=icb, hos=hos:
                                  g.dma_start(dwn[:, par, :, :],
                                              wdn_r[:, icb, :, hos * 512:(hos + 1) * 512]),
                                  deps=[ph2["dw_rd"].get(par, t_ln1_done)],
                                  inc="dq%d" % par)
                    if icb == 12 and hos == 0 and ph2.get("pend_out"):
                        for f in ph2.pop("pend_out"):
                            f()
                    for icw in range(4):
                        ic = icb * 4 + icw
                        for tc2 in range(2):
                            tc = tcb * 2 + tc2
                            d = []
                            if icb == 0 and icw == 0:
                                d = [t_fuse_all, t_dw,
                                     ph2["bank_rd"].get(tc2)]
                            elif icw == 0:
                                d = [t_dw]
                            t_mm = S.emit("tensor",
                                          lambda t, ic=ic, icw=icw, tc=tc, tc2=tc2, par=par,
                                          icb=icb:
                                          t.matmul(PS[2 + tc2][:, :],
                                                   fuse[:, ic, tc * 128:(tc + 1) * 128],
                                                   dwn[:, par, icw, :],
                                                   start=(icb == 0 and icw == 0), stop=False,
                                                   skip_group_check=True),
                                          deps=d, inc=(icw == 3 and tc2 == 1))
                    ph2["dw_rd"][par] = t_mm
                for tc2 in range(2):
                    tc = tcb * 2 + tc2
                    for i_dc in range(4):
                        dc = hos * 4 + i_dc
                        t_dn_stop[tc2] = S.emit(
                            "tensor",
                            lambda t, tc2=tc2, tc=tc, dc=dc, i_dc=i_dc, tt=tt:
                            t.matmul(PS[2 + tc2][:, i_dc * 128:(i_dc + 1) * 128],
                                     xbf[:, dc, tt * 512 + tc * 128: tt * 512 + (tc + 1) * 128],
                                     diag[:, dc, :],
                                     start=False, stop=(i_dc == 3),
                                     skip_group_check=True),
                            deps=([t_diag, t_ln1_done] if i_dc == 0 else []),
                            inc=(i_dc == 3))
                for tc2 in range(2):
                    prev_pair = (tcb - 1, tc2) if tcb > 0 else (1, tc2)
                    ph2["y2e"][(tc2, hos)] = S.emit(
                        "vector",
                        lambda v, tc2=tc2, hos=hos:
                        v.tensor_tensor(y2t[:, tc2, hos * 512:(hos + 1) * 512],
                                        PS[2 + tc2][:, :], s_bc[:, hos * 512:(hos + 1) * 512],
                                        ALU.mult),
                        deps=[t_dn_stop[tc2],
                              (ph2["outt"].get(prev_pair) if hos == 0 else None)],
                        inc=True)
                    ph2["bank_rd"][tc2] = ph2["y2e"][(tc2, hos)]
            for tc2 in range(2):
                tc = tcb * 2 + tc2
                mv = tc2 * 2
                sbase = tc2 * 8
                t_sq4 = None
                for seg in range(4):
                    t_sq4 = S.emit("scalar",
                                   lambda s, tc2=tc2, seg=seg, sbase=sbase:
                                   s.activation(PS[4][:, :],
                                                y2t[:, tc2, seg * 512:(seg + 1) * 512],
                                                AF.Square,
                                                accum_out=bnsc[:, sbase + seg:sbase + seg + 1]),
                                   deps=([ph2["y2e"][(tc2, s_)] for s_ in range(4)]
                                         + [ph2.get(("sq_rd", tc2))]) if seg == 0 else [],
                                   inc=(seg == 3))
                t_rsum = S.emit("vector",
                                lambda v, tc2=tc2, mv=mv:
                                v.reduce_sum(bnmv[:, mv:mv + 1], y2t[:, tc2, :],
                                             axis=mybir.AxisListType.X),
                                deps=[ph2.get(("lv", tc2))], inc=True)
                t_ng = S.emit("vector",
                              lambda v, sbase=sbase, mv=mv:
                              v.tensor_scalar(small2[:, sbase:sbase + 1],
                                              bnmv[:, mv:mv + 1], cp(6), None, ALU.mult),
                              deps=[], inc=True)
                t_ssq = S.emit("vector",
                               lambda v, sbase=sbase, mv=mv:
                               v.reduce_sum(bnmv[:, mv + 1:mv + 2],
                                            bnsc[:, sbase:sbase + 4],
                                            axis=mybir.AxisListType.X),
                               deps=[t_sq4], inc=True)
                ph2[("sq_rd", tc2)] = t_ssq
                t_m2 = S.emit("vector",
                              lambda v, sbase=sbase:
                              v.tensor_tensor(small2[:, sbase + 3:sbase + 4],
                                              small2[:, sbase:sbase + 1],
                                              small2[:, sbase:sbase + 1], ALU.mult),
                              deps=[], inc=True)
                t_var = S.emit("vector",
                               lambda v, sbase=sbase, mv=mv:
                               v.scalar_tensor_tensor(small2[:, sbase + 2:sbase + 3],
                                                      bnmv[:, mv + 1:mv + 2], cp(2),
                                                      small2[:, sbase + 3:sbase + 4],
                                                      ALU.mult, ALU.subtract),
                               deps=[], inc=True)
                t_lv = S.emit("scalar",
                              lambda s, sbase=sbase:
                              s.activation(small2[:, sbase + 4:sbase + 5],
                                           small2[:, sbase + 2:sbase + 3],
                                           AF.Ln, bias=cp(3)),
                              deps=[t_var], inc=True)
                ph2[("lv", tc2)] = t_lv
                t_rs = S.emit("scalar",
                              lambda s, sbase=sbase:
                              s.activation(small2[:, sbase + 1:sbase + 2],
                                           small2[:, sbase + 4:sbase + 5],
                                           AF.Exp, scale=cp(4)),
                              deps=[t_lv, ph2.get(("x2", tc2))], inc=True)
                t_dbg = None
                if debug:
                    t_dbg = S.emit("gpsimd",
                                   lambda g, tc2=tc2, tt=tt, tc=tc:
                                   g.dma_start(dbg_y2[tt * 4 + tc, :, :], y2t[:, tc2, :]),
                                   deps=[ph2["y2e"][(tc2, hh)] for hh in range(4)],
                                   inc="gq")
                t_x2 = S.emit("vector",
                              lambda v, tc2=tc2, sbase=sbase:
                              v.tensor_scalar(y2t[:, tc2, :], y2t[:, tc2, :],
                                              small2[:, sbase:sbase + 1],
                                              small2[:, sbase + 1:sbase + 2],
                                              ALU.add, ALU.mult),
                              deps=[t_ng, t_rs, t_dbg], inc=True)
                ph2[("x2", tc2)] = t_x2
                r0 = tt * 512 + tc * 128

                def mk_out(r0, tc2, t_x2, tcb):
                    def emit_out():
                        t_out = S.emit("gpsimd",
                                       lambda g, r0=r0, tc2=tc2:
                                       g.dma_start(out_d[r0:r0 + 128, :], y2t[:, tc2, :]),
                                       deps=[t_x2], inc="out%d" % tc2)
                        ph2["outt"][(tcb, tc2)] = t_out
                        t_out_toks.append(t_out)
                    return emit_out
                ph2.setdefault("pend_out", []).append(mk_out(r0, tc2, t_x2, tcb))
        ph2["dn_done_prev"] = ("tensor", S.cnt["tensor"])
        if tt == 3:
            for f in ph2.pop("pend_out", []):
                f()

    if debug:
        S.emit("gpsimd", lambda g: g.dma_start(dbg_fuse[...], fuse[...]),
               deps=[ph2["fuse"][63]], inc="gq")

    import contextlib
    with contextlib.ExitStack() as stack:
        sems = {}
        for nm in ("tensor", "scalar", "vector") + DMA_SEMS:
            sems[nm] = stack.enter_context(nc.semaphore("sem_" + nm))
        sems["gpsimd"] = sems["gq"]
        sems["sync"] = sems["sq"]
        n_out0 = S.cnt.get("out0", 0)
        n_out1 = S.cnt.get("out1", 0)
        with nc.Block() as block:

            @block.tensor
            def _(t):
                S.replay("tensor", t, sems, {})

            @block.scalar
            def _(s):
                S.replay("scalar", s, sems, {})

            @block.vector
            def _(v):
                S.replay("vector", v, sems, {})

            @block.gpsimd
            def _(g):
                S.replay("gpsimd", g, sems, {})

            @block.sync
            def _(s):
                S.replay("sync", s, sems, {})
                s.wait_ge(sems["out0"], n_out0)
                s.wait_ge(sems["out1"], n_out1)

    return nc


_CACHE = {}


def kernel(L0_attn_X, L0_attn_Wq, L0_attn_Wk, L0_attn_Wv, L0_attn_Wo,
           L0_ffn_w_gate, L0_ffn_w_up, L0_ffn_w_down,
           L0_ffn_s_gate, L0_ffn_s_up, L0_ffn_s_down):
    X = np.asarray(L0_attn_X, dtype=np.float32)
    B = X.shape[0]
    if "nc" not in _CACHE:
        _CACHE["nc"] = build_nc()
    nc = _CACHE["nc"]
    shared = {
        "wqt": np.ascontiguousarray(np.asarray(L0_attn_Wq, dtype=np.float32).T),
        "wkt": np.ascontiguousarray(np.asarray(L0_attn_Wk, dtype=np.float32).T),
        "wvt": np.ascontiguousarray(np.asarray(L0_attn_Wv, dtype=np.float32).T),
        "wot": np.ascontiguousarray(np.asarray(L0_attn_Wo, dtype=np.float32).T),
        "wg": np.ascontiguousarray(np.asarray(L0_ffn_w_gate, dtype=np.int8)),
        "wu": np.ascontiguousarray(np.asarray(L0_ffn_w_up, dtype=np.int8)),
        "wdn": np.ascontiguousarray(np.asarray(L0_ffn_w_down, dtype=np.int8)),
        "sgp": np.ascontiguousarray(
            np.asarray(L0_ffn_s_gate, dtype=np.float32).reshape(64, 128).T),
        "sup": np.ascontiguousarray(
            np.asarray(L0_ffn_s_up, dtype=np.float32).reshape(64, 128).T),
        "sdp": np.ascontiguousarray(
            np.asarray(L0_ffn_s_down, dtype=np.float32).reshape(16, 128).T),
        "sdr": np.ascontiguousarray(
            np.asarray(L0_ffn_s_down, dtype=np.float32).reshape(1, 2048)),
    }
    in_maps = []
    for c in range(B):
        m = dict(shared)
        m["xfm"] = np.ascontiguousarray(X[c].T)
        in_maps.append(m)
    res = run_bass_kernel_spmd(nc, in_maps, core_ids=list(range(B)))
    return np.stack([res.results[c]["out"] for c in range(B)], axis=0)
